# version 17
# baseline (speedup 1.0000x reference)
import numpy as np
import ml_dtypes
from contextlib import ExitStack

import concourse.bass as bass
import concourse.mybir as mybir
from concourse.bass_utils import run_bass_kernel_spmd

F32 = mybir.dt.float32
BF16 = mybir.dt.bfloat16
AF = mybir.ActivationFunctionType
ALU = mybir.AluOpType
AX = mybir.AxisListType

T = 2048
D = 1024
NH = 16
FF = 512
NE = 64
NT = T // 128
NDT = D // 128
NFB = FF // 128
EPS = 1e-5
EBLK = 3


class Sem:
    def __init__(self, ctx, nc, name):
        self.h = ctx.enter_context(nc.semaphore(name))
        self.n = 0


def build(dbg_names=(), phases="0123bc", a2_groups=NT // 2, a2_tiles=2, a2_sub=9):
    nc = bass.Bass("TRN2", target_bir_lowering=False)

    xin = nc.dram_tensor("x", [T, D], F32, kind="ExternalInput").ap()
    ln_g = nc.dram_tensor("ln_g", [1, D], F32, kind="ExternalInput").ap()
    ln_b = nc.dram_tensor("ln_b", [1, D], F32, kind="ExternalInput").ap()
    Wx_d = nc.dram_tensor("Wx", [D, D], F32, kind="ExternalInput").ap()
    Ww_d = nc.dram_tensor("Ww", [D, D], F32, kind="ExternalInput").ap()
    bw_d = nc.dram_tensor("bw", [128, NDT], F32, kind="ExternalInput").ap()
    Wk_d = nc.dram_tensor("Wk", [D, D], F32, kind="ExternalInput").ap()
    Wv_d = nc.dram_tensor("Wv", [D, D], F32, kind="ExternalInput").ap()
    Wr_d = nc.dram_tensor("Wr", [D, D], F32, kind="ExternalInput").ap()
    Wo_d = nc.dram_tensor("Wo", [D, D], F32, kind="ExternalInput").ap()
    cent_d = nc.dram_tensor("cent", [NE, D], F32, kind="ExternalInput").ap()
    ebias_d = nc.dram_tensor("ebias", [1, NE], F32, kind="ExternalInput").ap()
    Wsu_d = nc.dram_tensor("Wsu", [D, FF], F32, kind="ExternalInput").ap()
    Wsd_d = nc.dram_tensor("Wsd", [FF, D], F32, kind="ExternalInput").ap()
    Wup_d = nc.dram_tensor("Wup", [NE, D, FF], F32, kind="ExternalInput").ap()
    Wdn_d = nc.dram_tensor("Wdn", [NE, FF, D], F32, kind="ExternalInput").ap()
    identB_d = nc.dram_tensor("identB", [128, 128], BF16, kind="ExternalInput").ap()
    identF_d = nc.dram_tensor("identF", [128, 128], F32, kind="ExternalInput").ap()
    maskA_d = nc.dram_tensor("maskA", [128, 2048], BF16, kind="ExternalInput").ap()
    out_d = nc.dram_tensor("out", [T, D], F32, kind="ExternalOutput").ap()
    t1T_d = nc.dram_tensor("t1T_scr", [128, NDT * T], BF16, kind="Internal").ap()
    vsp_d = nc.dram_tensor("v_scr", [128, NT * D], BF16, kind="Internal").ap()
    o_d = nc.dram_tensor("o_scr", [128, NT * D], BF16, kind="Internal").ap()
    cwT_d = nc.dram_tensor("cwT_scr", [NE, T], F32, kind="Internal").ap()
    kT_d = nc.dram_tensor("kT_scr", [128, NDT * T], BF16, kind="Internal").ap()
    rT_d = nc.dram_tensor("rT_scr", [128, NDT * T], BF16, kind="Internal").ap()

    TL, VL, SL, GL, YL = [], [], [], [], []
    L = {"T": TL, "V": VL, "S": SL, "G": GL, "Y": YL}

    stack = ExitStack()
    E = stack.enter_context
    all_stacks = [stack]
    def mkstack():
        s = ExitStack()
        all_stacks.append(s)
        return s

    sT = Sem(stack, nc, "sT")
    sV = Sem(stack, nc, "sV")
    sS = Sem(stack, nc, "sS")
    sG = Sem(stack, nc, "sG")
    sO = Sem(stack, nc, "sO")
    qCst = Sem(stack, nc, "qCst")
    qRow = Sem(stack, nc, "qRow")
    qX = [Sem(stack, nc, f"qX{i}") for i in range(2)]
    qSp = [Sem(stack, nc, f"qSp{i}") for i in range(2)]
    qSv = [Sem(stack, nc, f"qSv{i}") for i in range(2)]
    qT1g = [Sem(stack, nc, f"qT1g{i}") for i in range(2)]
    qVt = [Sem(stack, nc, f"qVt{i}") for i in range(2)]
    qOst = [Sem(stack, nc, f"qOst{i}") for i in range(2)]
    qOl3 = [Sem(stack, nc, f"qOl3{i}") for i in range(2)]
    qW0 = Sem(stack, nc, "qW0")
    qW1 = Sem(stack, nc, "qW1")
    qWoS = Sem(stack, nc, "qWoS")
    qUp = [Sem(stack, nc, f"qUp{i}") for i in range(EBLK)]
    qDn = [Sem(stack, nc, f"qDn{i}") for i in range(EBLK)]
    qCw = Sem(stack, nc, "qCw")
    qBc = [Sem(stack, nc, f"qBc{i}") for i in range(EBLK)]
    qKt = [Sem(stack, nc, f"qKt{i}") for i in range(2)]
    qRt = [Sem(stack, nc, f"qRt{i}") for i in range(2)]
    qOut = [Sem(stack, nc, f"qOut{i}") for i in range(2)]

    def wait(e, sem, n):
        if n > 0:
            L[e].append(lambda g, h=sem.h, n=n: g.wait_ge(h, n))

    def barrier(e):
        wait(e, sT, sT.n)
        wait(e, sV, sV.n)
        wait(e, sS, sS.n)

    def dma(e, sem, out_ap, in_ap):
        sem.n += 16
        L[e].append(lambda g, o=out_ap, i=in_ap, h=sem.h: g.dma_start(o, i).then_inc(h, 16))
        return sem.n

    def mm(out_ap, lhs_ap, rhs_ap, start, stop, tp=None, inc=None):
        if inc is not None:
            inc.n += 1
        def f(t, o=out_ap, l=lhs_ap, r=rhs_ap, s=start, p=stop, tp=tp, inc=inc):
            kw = {} if tp is None else {"tile_position": tp}
            ins = t.matmul(o, l, r, start=s, stop=p, **kw)
            if inc is not None:
                ins.then_inc(inc.h, 1)
        TL.append(f)
        return None if inc is None else inc.n

    def trans(out_ap, in_ap, ident_ap, inc=None):
        if inc is not None:
            inc.n += 1
        def f(t, o=out_ap, i=in_ap, d=ident_ap, inc=inc):
            ins = t.transpose(o, i, d)
            if inc is not None:
                ins.then_inc(inc.h, 1)
        TL.append(f)
        return None if inc is None else inc.n

    def op(e, name, *args, inc=None, chain=True, **kwargs):
        if e == "V":
            if chain and sV.n > 0:
                L["V"].append(lambda g, h=sV.h, n=sV.n: g.wait_ge(h, n))
            inc = sV
        if e == "G" and name not in ("dma_start",):
            if sG.n > 0:
                L["G"].append(lambda g, h=sG.h, n=sG.n: g.wait_ge(h, n))
            inc = sG
        if inc is not None:
            inc.n += 1
        def f(g, name=name, args=args, kwargs=kwargs, inc=inc):
            ins = getattr(g, name)(*args, **kwargs)
            if inc is not None:
                ins.then_inc(inc.h, 1)
        L[e].append(f)
        return None if inc is None else inc.n

    dbg_info = {}

    def dbg(name, ap_, dt):
        if name in dbg_names:
            shp = [int(s) for s in ap_.shape]
            o = nc.dram_tensor("dbg_" + name, shp, dt, kind="ExternalOutput").ap()
            dbg_info[name] = dt
            wait("Y", sV, sV.n)
            wait("Y", sS, sS.n)
            wait("Y", sT, sT.n)
            dma("Y", sO, o, ap_)

    xnT = E(nc.sbuf_tensor("xnT", [128, NDT * T], BF16))
    cwT = E(nc.sbuf_tensor("cwT", [NE, T], F32))
    identB = E(nc.sbuf_tensor("identB_s", [128, 128], BF16))
    identF = E(nc.sbuf_tensor("identF_s", [128, 128], F32))
    maskA = E(nc.sbuf_tensor("maskA_s", [128, 2048], BF16))
    bw_sb = E(nc.sbuf_tensor("bw_sb", [128, NDT], F32))
    ones1 = E(nc.sbuf_tensor("ones1", [1, 128], F32))
    zeros128 = E(nc.sbuf_tensor("zeros128", [128, 128], F32))
    mask2 = E(nc.sbuf_tensor("mask2", [128, 2], F32))

    dma("G", qCst, identB[:, :], identB_d)
    dma("G", qCst, identF[:, :], identF_d)
    dma("G", qCst, maskA[:, :], maskA_d)
    dma("G", qCst, bw_sb[:, :], bw_d)
    op("G", "memset", ones1[:, :], 1.0)
    op("G", "memset", zeros128[:, :], 0.0)
    op("G", "memset", mask2[0:64, 0:1], 1.0)
    op("G", "memset", mask2[64:128, 0:1], 0.0)
    op("G", "memset", mask2[0:64, 1:2], 0.0)
    op("G", "memset", mask2[64:128, 1:2], 1.0)
    G_consts = qCst.n
    wait("T", sG, sG.n)
    wait("V", sG, sG.n)
    wait("S", sG, sG.n)

    p0 = mkstack()
    E0r = p0.enter_context
    E0 = lambda cm: E0r(cm)
    s01 = mkstack()
    xn2T = s01.enter_context(nc.sbuf_tensor("xn2T", [128, NDT * T], BF16, side="right"))
    xnloT = E0(nc.sbuf_tensor("xnloT", [128, NDT * T], BF16, side="right"))
    g_bc = E0(nc.sbuf_tensor("g_bc", [128, D], F32, side="right"))
    b_bc = E0(nc.sbuf_tensor("b_bc", [128, D], F32, side="right"))
    eb_bc = E0(nc.sbuf_tensor("eb_bc", [128, NE], F32, side="right"))
    grow = E0(nc.sbuf_tensor("grow", [1, D], F32, side="right"))
    brow = E0(nc.sbuf_tensor("brow", [1, D], F32, side="right"))
    erow = E0(nc.sbuf_tensor("erow", [1, NE], F32, side="right"))
    cent_sb = E0(nc.sbuf_tensor("cent_sb", [NE, D], F32, side="right"))
    cent_hi = E0(nc.sbuf_tensor("cent_hi", [NE, D], BF16, side="right"))
    cent_lo = E0(nc.sbuf_tensor("cent_lo", [NE, D], BF16, side="right"))
    centT_hi = E0(nc.sbuf_tensor("centT_hi", [128, NDT * NE], BF16, side="right"))
    centT_lo = E0(nc.sbuf_tensor("centT_lo", [128, NDT * NE], BF16, side="right"))
    affT_sb = E0(nc.sbuf_tensor("affT_sb", [64, T], F32, side="right"))
    cols = [E0(nc.sbuf_tensor(f"cols{i}", [128, 16], F32, side="right")) for i in range(2)]
    aff_sb = E0(nc.sbuf_tensor("aff_sb", [128, NT * NE], F32, side="right"))
    bias_t = E0(nc.sbuf_tensor("bias_t", [128, NE], F32, side="right"))
    m8 = E0(nc.sbuf_tensor("m8", [128, 64], F32, side="right"))
    sgm = E0(nc.sbuf_tensor("sgm", [128, NE], F32, side="right"))
    wsel = E0(nc.sbuf_tensor("wsel", [128, NE], F32, side="right"))
    cw_t = E0(nc.sbuf_tensor("cw_t", [128, NE], F32, side="right"))

    p0ln = mkstack()
    Eln = p0ln.enter_context
    xtl = [Eln(nc.sbuf_tensor(f"xtl{i}", [128, D], F32, side="right")) for i in range(2)]
    lnt = [Eln(nc.sbuf_tensor(f"lnt{i}", [128, D], F32, side="right")) for i in range(2)]
    ln2t = [Eln(nc.sbuf_tensor(f"ln2t{i}", [128, D], F32, side="right")) for i in range(2)]
    lnb = [Eln(nc.sbuf_tensor(f"lnb{i}", [128, D], BF16, side="right")) for i in range(2)]
    lnlob = [Eln(nc.sbuf_tensor(f"lnlob{i}", [128, D], BF16, side="right")) for i in range(2)]
    ln2b = [Eln(nc.sbuf_tensor(f"ln2b{i}", [128, D], BF16, side="right")) for i in range(2)]

    pT0 = [E0(nc.psum_tensor(f"pT0{i}", [128, 1024], BF16)) for i in range(2)]
    pP0f = E0(nc.psum_tensor("pP0f", [128, 1024], F32))
    pA_st = mkstack()
    pAff = pA_st.enter_context(nc.psum_tensor("pAff", [64, T], F32))

    dma("Y", qRow, grow[:, :], ln_g)
    dma("Y", qRow, brow[:, :], ln_b)
    dma("Y", qRow, erow[:, :], ebias_d)
    rows_y = dma("Y", qRow, cent_sb[:, :], cent_d)
    wait("T", qRow, rows_y)
    wait("T", qCst, G_consts)
    for (dst, src, n) in ((g_bc, grow, D), (b_bc, brow, D), (eb_bc, erow, NE)):
        last = None
        for j in range(0, n, 512):
            w_ = min(512, n - j)
            last = mm(pP0f[:, j:j + w_], ones1[:, :], src[:, j:j + w_], True, True,
                      inc=(sT if j + 512 >= n else None))
        wait("V", sT, last)
        ev = op("V", "tensor_copy", dst[:, 0:n], pP0f[:, 0:n], inc=sV)
        wait("T", sV, ev)

    wait("V", qRow, rows_y)
    op("V", "tensor_copy", cent_hi[:, :], cent_sb[:, :])
    chl = op("V", "tensor_tensor", cent_lo[:, :], cent_sb[:, :], cent_hi[:, :],
             ALU.subtract, inc=sV)
    wait("T", sV, chl)
    for (src, dstT) in ((cent_hi, centT_hi), (cent_lo, centT_lo)):
        last = None
        for ct in range(NDT):
            last = trans(pT0[0][:, ct * 64:(ct + 1) * 64], src[0:64, ct * 128:(ct + 1) * 128],
                         identB[0:64, 0:64], inc=(sT if ct == NDT - 1 else None))
        wait("V", sT, last)
        ev = op("V", "tensor_copy", dstT[:, :], pT0[0][:, 0:NDT * 64], inc=sV)
        wait("T", sV, ev)

    sW0 = mkstack()
    Wsl0 = sW0.enter_context(nc.sbuf_tensor("Wsl0", [128, NDT * 1024], BF16))

    def load_w(dram_ap, slot_t, qsem, twait=0):
        wait("G", sT, twait)
        return dma("G", qsem, slot_t[:, :].rearrange("p (c u) -> p c u", u=1024),
                   dram_ap.rearrange("(c p) u -> p c u", p=128))

    wx_ld = load_w(Wx_d, Wsl0, qW0)

    wait("S", qCst, G_consts)
    rD = 1.0 / D
    ln_stats_S = {}
    ln_stats_V = {}
    split_v = {}
    tr_ev = {}
    tr_T_last = {-1: 0, -2: 0}
    x_ln_y = {}
    g3 = 0

    def ln_A(tt):
        i2 = tt % 2
        xb, lt, cl = xtl[i2], lnt[i2], cols[i2]
        if tt >= 2:
            wait("Y", sV, split_v[tt - 2])
        x_ln_y[tt] = dma("Y", qX[i2], xb[:, :], xin[tt * 128:(tt + 1) * 128, :])
        wait("S", qX[i2], x_ln_y[tt])
        if tt >= 2:
            wait("S", sV, split_v[tt - 2])
        op("S", "activation", lt[:, :], xb[:, :], AF.Identity, accum_out=cl[:, 0:1])
        op("S", "activation", ln2t[tt % 2][:, :], xb[:, :], AF.Square,
           accum_out=cl[:, 1:2])
        ss = op("S", "activation", cl[:, 11:12], cl[:, 1:2], AF.Identity, inc=sS)
        wait("V", sS, ss)
        op("V", "tensor_scalar", cl[:, 2:3], cl[:, 0:1], rD, None, ALU.mult)
        op("V", "tensor_scalar", cl[:, 3:4], cl[:, 1:2], rD, None, ALU.mult)
        op("V", "tensor_scalar", cl[:, 4:5], cl[:, 2:3], cl[:, 2:3], None, ALU.mult)
        op("V", "tensor_tensor", cl[:, 5:6], cl[:, 3:4], cl[:, 4:5], ALU.subtract)
        vv = op("V", "tensor_scalar", cl[:, 6:7], cl[:, 5:6], EPS, None, ALU.add, inc=sV)
        wait("S", sV, vv)
        op("S", "activation", cl[:, 8:9], cl[:, 6:7], AF.Ln)
        s2 = op("S", "activation", cl[:, 9:10], cl[:, 8:9], AF.Exp, scale=-0.5, inc=sS)
        ln_stats_S[tt] = s2

    def ln_B(tt):
        nonlocal g3
        i2 = tt % 2
        xb, lt, l2t, lb, llo, l2b, cl = (xtl[i2], lnt[i2], ln2t[i2], lnb[i2],
                                         lnlob[i2], ln2b[i2], cols[i2])
        wait("V", sS, ln_stats_S[tt])
        op("V", "reciprocal", cl[:, 7:8], cl[:, 6:7])
        vv = op("V", "tensor_scalar", cl[:, 10:11], cl[:, 5:6], cl[:, 7:8], EPS,
                ALU.mult, ALU.add, inc=sV)
        wait("S", sV, vv)
        op("S", "activation", cl[:, 12:13], cl[:, 10:11], AF.Ln)
        s2 = op("S", "activation", cl[:, 13:14], cl[:, 12:13], AF.Exp, scale=-0.5, inc=sS)
        wait("V", sS, s2)
        op("V", "tensor_scalar", cl[:, 14:15], cl[:, 9:10], cl[:, 13:14], None, ALU.mult)
        op("V", "tensor_scalar", cl[:, 8:9], cl[:, 2:3], cl[:, 9:10], -1.0, ALU.mult, ALU.mult)
        op("V", "tensor_scalar", cl[:, 15:16], cl[:, 2:3], cl[:, 14:15], -1.0,
           ALU.mult, ALU.mult)
        wait("V", sT, tr_T_last[tt - 2])
        op("V", "tensor_scalar", lt[:, :], xb[:, :], cl[:, 9:10], cl[:, 8:9],
           ALU.mult, ALU.add)
        op("V", "tensor_scalar", l2t[:, :], xb[:, :], cl[:, 14:15], cl[:, 15:16],
           ALU.mult, ALU.add)
        op("V", "tensor_copy", lb[:, :], lt[:, :])
        op("V", "tensor_tensor", llo[:, :], lt[:, :], lb[:, :], ALU.subtract)
        op("V", "tensor_tensor", l2t[:, :], l2t[:, :], g_bc[:, :], ALU.mult)
        split_v[tt] = op("V", "tensor_tensor", l2b[:, :], l2t[:, :], b_bc[:, :],
                         ALU.add, inc=sV)
        wait("T", sV, split_v[tt])
        for (src_, dstT) in ((lb, xnT), (llo, xnloT), (l2b, xn2T)):
            pp = pT0[g3 % 2]
            if g3 >= 2:
                wait("T", sS, tr_ev[g3 - 2])
            last = None
            for c in range(NDT):
                last = trans(pp[:, c * 128:(c + 1) * 128], src_[:, c * 128:(c + 1) * 128],
                             identB[:, :], inc=(sT if c == NDT - 1 else None))
            tr_T_last[tt] = last
            wait("S", sT, last)
            o_ap = dstT[:, :].rearrange("p (c u) -> p c u", u=T)[:, :, tt * 128:(tt + 1) * 128]
            i_ap = pp[:, :].rearrange("p (c u) -> p c u", u=128)
            tr_ev[g3] = op("S", "activation", o_ap, i_ap, AF.Copy, inc=sS)
            g3 += 1

    ln_A(0)
    for tt in range(NT):
        if tt + 1 < NT:
            ln_A(tt + 1)
        ln_B(tt)

    dbg("xnT", xnT[:, :], BF16)
    dbg("xn2T", xn2T[:, :], BF16)
    p0ln.close()
    sW1 = mkstack()
    Wsl1 = sW1.enter_context(nc.sbuf_tensor("Wsl1", [128, NDT * 1024], BF16))
    sSPL = mkstack()
    spl = [sSPL.enter_context(nc.sbuf_tensor(f"spl{i}", [128, 2048], BF16)) for i in range(2)]
    wait("G", sV, sV.n)
    wait("G", sS, sS.n)
    wait("G", qX[0], qX[0].n)
    wait("G", qX[1], qX[1].n)
    wk_ld = load_w(Wk_d, Wsl1, qW1)

    wait("T", sV, sV.n)
    wait("T", sS, sS.n)
    aff_last = None
    for ts in range(4):
        first = True
        for (cT, xT) in ((centT_hi, xnT), (centT_hi, xnloT), (centT_lo, xnT)):
            for ct in range(NDT):
                stop = (cT is centT_lo) and (ct == NDT - 1)
                aff_last = mm(pAff[:, ts * 512:(ts + 1) * 512],
                              cT[:, ct * 64:(ct + 1) * 64],
                              xT[:, ct * T + ts * 512: ct * T + (ts + 1) * 512],
                              first, stop,
                              inc=(sT if (stop and ts == 3) else None))
                first = False
    wait("V", sT, aff_last)
    aff_ev = op("V", "tensor_copy", affT_sb[:, :], pAff[:, :], inc=sV)

    pA_st.close()
    a1p = mkstack()
    pWs = [a1p.enter_context(nc.psum_tensor(f"pW{i}", [128, 512], F32)) for i in range(2)]
    pj_ev = {-1: (sV, 0), -2: (sV, 0)}
    pj_y = {0: 0, 1: 0}
    vj_y = {0: 0, 1: 0}
    pj_r = [0]

    def proj_dpart(slot_t, w_ready, out_dram, sigmoid=False):
        wait("T", w_ready[0], w_ready[1])
        wait("T", sV, sV.n)
        wait("T", sS, sS.n)
        eng = "S" if sigmoid else "V"
        sem = sS if sigmoid else sV
        for dt in range(NDT):
            sb = spl[dt % 2]
            for ts in range(4):
                r = pj_r[0]
                pj_r[0] += 1
                pp = pWs[r % 2]
                wait("T", *pj_ev[r - 2])
                last = None
                for ct in range(NDT):
                    lhs = slot_t[:, ct * 1024 + dt * 128: ct * 1024 + dt * 128 + 128]
                    last = mm(pp[:, :], lhs,
                              xn2T[:, ct * T + ts * 512: ct * T + (ts + 1) * 512],
                              ct == 0, ct == NDT - 1,
                              inc=(sT if ct == NDT - 1 else None))
                if ts == 0:
                    wait(eng, qSp[dt % 2], pj_y[dt % 2])
                wait(eng, sT, last)
                if sigmoid:
                    pj_ev[r] = (sS, op("S", "activation", sb[:, ts * 512:(ts + 1) * 512],
                                       pp[:, :], AF.Sigmoid, inc=sS))
                else:
                    pj_ev[r] = (sV, op("V", "tensor_copy", sb[:, ts * 512:(ts + 1) * 512],
                                       pp[:, :], inc=sV))
            wait("Y", *pj_ev[pj_r[0] - 1])
            pj_y[dt % 2] = dma("Y", qSp[dt % 2], out_dram[:, dt * T:(dt + 1) * T],
                               sb[:, :])

    proj_dpart(Wsl1, (qW1, wk_ld), kT_d)
    dbg("wsl1", Wsl1[:, :], BF16)
    proj_dpart(Wsl0, (qW0, wx_ld), t1T_d)
    wr_ld = load_w(Wr_d, Wsl0, qW0, twait=sT.n)
    wv_ld = load_w(Wv_d, Wsl1, qW1, twait=sT.n)

    cwtr_ev = {}
    for tt in range(NT):
        half = (tt % 2) * 512
        wait("T", sV, aff_ev)
        if tt >= 2:
            wait("T", sV, cwtr_ev[tt - 2])
        atr = trans(pP0f[:, half:half + 64], affT_sb[:, tt * 128:(tt + 1) * 128],
                    identF[0:64, 0:64], inc=sT)
        wait("V", sT, atr)
        op("V", "tensor_copy", aff_sb[:, tt * NE:(tt + 1) * NE], pP0f[:, half:half + 64])
        a_t = aff_sb[:, tt * NE:(tt + 1) * NE]
        op("V", "tensor_tensor", bias_t[:, :], a_t, eb_bc[:, :], ALU.add)
        mx = op("V", "max", out=m8[:, 0:8], in_=bias_t[:, :], inc=sV)
        wait("V", sV, mx)
        ws_ = op("V", "tensor_scalar", wsel[:, :], bias_t[:, :], m8[:, 7:8], None,
                 ALU.is_ge, inc=sV)
        wait("S", sV, ws_)
        sg_ = op("S", "activation", sgm[:, :], a_t, AF.Sigmoid, inc=sS)
        wait("V", sS, sg_)
        op("V", "tensor_tensor", wsel[:, :], wsel[:, :], sgm[:, :], ALU.mult)
        op("V", "reduce_sum", cols[0][:, 0:1], wsel[:, :], AX.X)
        op("V", "tensor_scalar", cols[0][:, 1:2], cols[0][:, 0:1], 1e-8, None, ALU.add)
        op("V", "reciprocal", cols[0][:, 2:3], cols[0][:, 1:2])
        cwv = op("V", "tensor_scalar", cw_t[:, :], wsel[:, :], cols[0][:, 2:3], None,
                 ALU.mult, inc=sV)
        wait("T", sV, cwv)
        ctr = trans(pP0f[0:64, half + 64:half + 192], cw_t[:, :], identF[:, :], inc=sT)
        wait("V", sT, ctr)
        cwtr_ev[tt] = op("V", "tensor_copy", cwT[:, tt * 128:(tt + 1) * 128],
                         pP0f[0:64, half + 64:half + 192], inc=sV)

    wait("Y", sV, sV.n)
    cw_store = dma("Y", qRow, cwT_d, cwT[:, :])
    dbg("aff", aff_sb[:, :], F32)
    dbg("cwT", cwT[:, :], F32)


    proj_dpart(Wsl0, (qW0, wr_ld), rT_d, sigmoid=True)

    wait("T", qW1, wv_ld)
    wait("T", sS, sS.n)
    wait("T", sV, sV.n)
    wait("V", qSp[0], pj_y[0])
    wait("V", qSp[1], pj_y[1])
    for tt in range(NT):
        sb = spl[tt % 2]
        for ts in range(2):
            r = pj_r[0]
            pj_r[0] += 1
            pp = pWs[r % 2]
            wait("T", *pj_ev[r - 2])
            last = None
            for ct in range(NDT):
                lhs = xn2T[:, ct * T + tt * 128: ct * T + (tt + 1) * 128]
                last = mm(pp[:, :], lhs,
                          Wsl1[:, ct * 1024 + ts * 512: ct * 1024 + (ts + 1) * 512],
                          ct == 0, ct == NDT - 1,
                          inc=(sT if ct == NDT - 1 else None))
            if ts == 0:
                wait("V", qSv[tt % 2], vj_y[tt % 2])
            wait("V", sT, last)
            pj_ev[r] = (sV, op("V", "tensor_copy", sb[:, ts * 512:(ts + 1) * 512],
                               pp[:, :], inc=sV))
        wait("Y", *pj_ev[pj_r[0] - 1])
        vj_y[tt % 2] = dma("Y", qSv[tt % 2], vsp_d[:, tt * 1024:(tt + 1) * 1024],
                           sb[:, 0:1024])

    ww_ld = load_w(Ww_d, Wsl0, qW0, twait=sT.n)

    dbg("t1Td", t1T_d[:, :], BF16)
    dbg("kTd", kT_d[:, :], BF16)
    dbg("rTd", rT_d[:, :], BF16)
    dbg("vspd", vsp_d[:, :], BF16)
    wait("T", sV, sV.n)
    wait("T", sS, sS.n)
    a1p.close()
    p0.close()
    s01.close()
    sSPL.close()
    sW1.close()

    if "2" not in phases:
        return _finish(nc, all_stacks, L, sO, dbg_info)

    a2 = mkstack()
    E2 = a2.enter_context
    o_tl = [E2(nc.sbuf_tensor(f"o_tl{i}", [128, 1024], BF16, side="right")) for i in range(2)]
    t1g = [E2(nc.sbuf_tensor(f"t1g{i}", [128, NDT * 256], BF16, side="right")) for i in range(2)]
    v_tl = [E2(nc.sbuf_tensor(f"v_tl{i}", [128, 1024], BF16, side="right")) for i in range(2)]
    kt_sl = [E2(nc.sbuf_tensor(f"kt_sl{i}", [128, 1024], BF16, side="right")) for i in range(2)]
    rt_sl = [E2(nc.sbuf_tensor(f"rt_sl{i}", [128, 1024], BF16, side="right")) for i in range(2)]
    STg = [E2(nc.sbuf_tensor(f"STg{i}", [128, NDT * 256], F32, side="right")) for i in range(2)]
    sp_u = [E2(nc.sbuf_tensor(f"sp_u{i}", [128, 512], F32, side="right")) for i in range(2)]
    sp_e = [E2(nc.sbuf_tensor(f"sp_e{i}", [128, 512], F32, side="right")) for i in range(2)]
    sp_l = [E2(nc.sbuf_tensor(f"sp_l{i}", [128, 512], F32, side="right")) for i in range(2)]
    sp_r = [E2(nc.sbuf_tensor(f"sp_r{i}", [128, 512], F32, side="right")) for i in range(2)]
    sftp = [E2(nc.sbuf_tensor(f"sftp{i}", [128, 512], F32, side="right")) for i in range(2)]
    bcols = [E2(nc.sbuf_tensor(f"bcols{i}", [128, 64], F32, side="right")) for i in range(2)]
    E_eb = [E2(nc.sbuf_tensor(f"E_eb{i}", [128, 1024], BF16, side="right")) for i in range(2)]
    E_kh = [E2(nc.sbuf_tensor(f"E_kh{i}", [128, 1024], BF16, side="right")) for i in range(2)]
    E_kt = [E2(nc.sbuf_tensor(f"E_kt{i}", [128, 1024], BF16, side="right")) for i in range(2)]
    E_r0 = [E2(nc.sbuf_tensor(f"E_r0{i}", [128, 1024], BF16, side="right")) for i in range(2)]
    E_r1 = [E2(nc.sbuf_tensor(f"E_r1{i}", [128, 512], BF16, side="right")) for i in range(2)]
    clampb = E2(nc.sbuf_tensor("clampb", [128, 2048], BF16, side="right"))
    khT = [[E2(nc.sbuf_tensor(f"khT{i}{j}", [128, 1024], BF16, side="right")) for j in range(2)]
           for i in range(2)]
    ktlT = [[E2(nc.sbuf_tensor(f"ktlT{i}{j}", [128, 1024], BF16, side="right")) for j in range(2)]
            for i in range(2)]
    rt0T = [[E2(nc.sbuf_tensor(f"rt0T{i}{j}", [128, 1024], BF16, side="right")) for j in range(2)]
            for i in range(2)]
    rt1T = [[E2(nc.sbuf_tensor(f"rt1T{i}{j}", [128, 512], BF16, side="right")) for j in range(2)]
            for i in range(2)]
    rhT = [[E2(nc.sbuf_tensor(f"rhT{i}{j}", [128, 1024], BF16, side="right")) for j in range(2)]
           for i in range(2)]
    khtp = [E2(nc.sbuf_tensor(f"khtp{j}", [128, 1024], BF16, side="right")) for j in range(2)]
    Am = E2(nc.sbuf_tensor("Am", [128, 2048], BF16, side="right"))
    state = E2(nc.sbuf_tensor("state", [128, 512], F32, side="right"))
    state_bf = E2(nc.sbuf_tensor("state_bf", [128, 512], BF16, side="right"))
    edec = [E2(nc.sbuf_tensor(f"edec{i}", [128, 8], F32, side="right")) for i in range(2)]

    pz2 = [E2(nc.psum_tensor(f"pz{i}", [128, 512], F32)) for i in range(2)]
    pAtr = E2(nc.psum_tensor("pAtr", [128, 1024], F32))
    pOut = E2(nc.psum_tensor("pOut", [128, 1024], F32))
    pSup = E2(nc.psum_tensor("pSup", [128, 512], F32))
    pKtp = pSup[:, :].bitcast(BF16)

    barrier("Y")
    wait("Y", qSp[0], qSp[0].n)
    wait("Y", qSp[1], qSp[1].n)
    wait("Y", qSv[0], qSv[0].n)
    wait("Y", qSv[1], qSv[1].n)
    wait("V", qSp[0], qSp[0].n)
    wait("V", qSp[1], qSp[1].n)
    wait("V", qSv[0], qSv[0].n)
    wait("V", qSv[1], qSv[1].n)
    op("V", "memset", state[:, :], 0.0)
    op("V", "memset", state_bf[:, :], 0.0)
    for i in range(2):
        for j in range(2):
            op("V", "memset", ktlT[i][j][:, :], 0.0)
    a2_init = op("V", "memset", pAtr[:, :], 0.0, inc=sV)
    wait("T", sV, a2_init)

    ST3g = [STg[i][:, :].rearrange("p (c u) -> p c u", u=256) for i in range(2)]
    kT_r = kT_d.rearrange("p (c u) -> p c u", u=T)
    rT_r = rT_d.rearrange("p (c u) -> p c u", u=T)

    z_T = {}
    z_S = {-1: 0, -2: 0}
    scan_V = {-1: 0, -2: 0}
    last_oev = 0
    last_stV = [a2_init]
    last_cbv = [a2_init]
    last_khtpS = 0
    av_T = {-1: 0, -2: 0}
    ktv_tile = {}
    amm0_tile = {}
    amm1_tile = {}
    cb0_tile = {}
    m0_tile = {}
    m1_tile = {}
    sup_tile = {}
    tile_T = {-1: 0, -2: 0}
    scS_tile = {-1: 0, -2: 0}
    v1_tile = {}
    stV_tile = {-1: 0, -2: 0}
    o_y = {}
    t1_y = {}
    vt_y = {}
    kt_y = {}
    rt_y = {}
    slice_free_V = {-2: 0, -1: 0}
    t1_free_T = {}
    v_free_T = {}

    def load_slices(tt):
        if tt >= NT:
            return
        wait("Y", sV, slice_free_V[tt - 2])
        kt_y[tt] = dma("Y", qKt[tt % 2], kt_sl[tt % 2][:, :].rearrange("p (c u) -> p c u", u=128),
                       kT_r[:, :, tt * 128:(tt + 1) * 128])
        rt_y[tt] = dma("Y", qRt[tt % 2], rt_sl[tt % 2][:, :].rearrange("p (c u) -> p c u", u=128),
                       rT_r[:, :, tt * 128:(tt + 1) * 128])

    def emit_z_step(g_, dtp):
        gp = g_ % 2
        stepp = g_ * (NDT // 2) + dtp
        par = stepp % 2
        ppz = pz2[par]
        if dtp == 0:
            if g_ >= 2:
                wait("Y", sT, t1_free_T[g_ - 2])
            t1_y[g_] = dma("Y", qT1g[gp], t1g[gp][:, :].rearrange("p (c u) -> p c u", u=256),
                           t1T_d.rearrange("p (c u) -> p c u", u=T)[:, :, g_ * 256:(g_ + 1) * 256])
        wait("T", sS, z_S[stepp - 2])
        wait("T", qT1g[gp], t1_y[g_])
        if stepp == 0:
            wait("T", qW0, ww_ld)
        last = None
        for half in range(2):
            dt = 2 * dtp + half
            for ct in range(NDT):
                last = mm(ppz[:, half * 256: half * 256 + 256],
                          Wsl0[:, ct * 1024 + dt * 128: ct * 1024 + dt * 128 + 128],
                          t1g[gp][:, ct * 256:(ct + 1) * 256], ct == 0, ct == NDT - 1,
                          inc=(sT if (half == 1 and ct == NDT - 1) else None))
        z_T[stepp] = last
        wait("S", sT, last)
        wait("S", sV, scan_V[stepp - 2])
        for half in range(2):
            dt = 2 * dtp + half
            bwc = bw_sb[:, dt:dt + 1]
            sl = slice(half * 256, half * 256 + 256)
            op("S", "activation", sp_u[par][:, sl], ppz[:, sl], AF.Abs, bias=bwc)
            op("S", "activation", sp_e[par][:, sl], sp_u[par][:, sl], AF.Exp, scale=-1.0)
            op("S", "activation", sp_l[par][:, sl], sp_e[par][:, sl], AF.Ln, bias=1.0)
            z_S[stepp] = op("S", "activation", sp_r[par][:, sl], ppz[:, sl], AF.Relu,
                            bias=bwc, inc=(sS if half == 1 else None))
        if dtp == NDT // 2 - 1:
            t1_free_T[g_] = z_T[stepp]

    def emit_scans(g_, dtp):
        gp = g_ % 2
        stepp = g_ * (NDT // 2) + dtp
        par = stepp % 2
        wait("V", sS, z_S[stepp])
        if g_ >= 2 and dtp == 0:
            wait("V", sS, scS_tile[2 * (g_ - 2) + 1])
        first = True
        for half in range(2):
            dt = 2 * dtp + half
            for tl in range(2):
                sl = slice(half * 256 + tl * 128, half * 256 + (tl + 1) * 128)
                scan_V[stepp] = op("V", "tensor_tensor_scan",
                                   STg[gp][:, dt * 256 + tl * 128: dt * 256 + (tl + 1) * 128],
                                   sp_r[par][:, sl], sp_l[par][:, sl], 0.0,
                                   ALU.add, ALU.add, chain=first,
                                   inc=(sV if (half == 1 and tl == 1) else None))
                first = False

    def argsV(tt):
        g_ = tt // 2
        tl = tt % 2
        tp_ = tt % 2
        gp = g_ % 2
        ST3 = ST3g[gp]
        wait("V", sV, scan_V[g_ * (NDT // 2) + NDT // 2 - 1])
        wait("V", sS, scS_tile[tt - 2])
        op("V", "tensor_copy", bcols[tp_][:, 0:8], ST3[:, :, tl * 128 + 31: tl * 128 + 32])
        op("V", "tensor_copy", bcols[tp_][:, 8:16], ST3[:, :, tl * 128 + 95: tl * 128 + 96],
           chain=False)
        op("V", "tensor_copy", bcols[tp_][:, 16:24],
           ST3[:, :, tl * 128 + 127: tl * 128 + 128], chain=False)
        v1 = op("V", "tensor_scalar", bcols[tp_][:, 32:56], bcols[tp_][:, 0:24],
                -1.0, None, ALU.mult, inc=sV)
        v1_tile[tt] = v1

    def expsS(tt):
        g_ = tt // 2
        tl = tt % 2
        tp_ = tt % 2
        gp = g_ % 2
        ST3 = ST3g[gp]
        v1 = v1_tile[tt]
        wait("S", sV, v1)
        wait("S", sV, stV_tile[tt - 2])
        op("S", "activation", edec[tp_][:, :], bcols[tp_][:, 16:24], AF.Exp, scale=-1.0)
        op("S", "activation", E_eb[tp_][:, :].rearrange("p (c u) -> p c u", u=128),
           ST3[:, :, tl * 128:(tl + 1) * 128], AF.Exp, scale=-1.0)
        for dt in range(NDT):
            op("S", "activation", E_kh[tp_][:, dt * 128:(dt + 1) * 128],
               ST3[:, dt, tl * 128:(tl + 1) * 128], AF.Exp,
               bias=bcols[tp_][:, 48 + dt:49 + dt])
        for half in range(2):
            for dt in range(NDT):
                op("S", "activation",
                   E_kt[tp_][:, dt * 128 + half * 64: dt * 128 + half * 64 + 64],
                   ST3[:, dt, tl * 128 + half * 64: tl * 128 + half * 64 + 64],
                   AF.Exp, bias=bcols[tp_][:, 32 + half * 8 + dt: 33 + half * 8 + dt])
        scS = None
        for dt in range(NDT):
            op("S", "activation", E_r0[tp_][:, dt * 128:(dt + 1) * 128],
               ST3[:, dt, tl * 128:(tl + 1) * 128], AF.Exp, scale=-1.0,
               bias=bcols[tp_][:, 0 + dt:1 + dt])
            scS = op("S", "activation", E_r1[tp_][:, dt * 64:(dt + 1) * 64],
                     ST3[:, dt, tl * 128 + 64:tl * 128 + 128], AF.Exp, scale=-1.0,
                     bias=bcols[tp_][:, 8 + dt:9 + dt],
                     inc=(sS if dt == NDT - 1 else None))
        scS_tile[tt] = scS

    def stageV2(tt):
        tp_ = tt % 2
        wait("V", sS, scS_tile[tt])
        wait("V", qKt[tp_], kt_y[tt])
        wait("V", qRt[tp_], rt_y[tt])
        wait("V", sT, av_T[tt - 2])
        for par in range(2):
            op("V", "scalar_tensor_tensor", rhT[tp_][par][:, :],
               rt_sl[tp_][:, :], mask2[:, par:par + 1], E_eb[tp_][:, :],
               ALU.mult, ALU.mult, chain=(par == 0))
            op("V", "scalar_tensor_tensor", khT[tp_][par][:, :],
               kt_sl[tp_][:, :], mask2[:, par:par + 1], E_kh[tp_][:, :],
               ALU.mult, ALU.mult, chain=False)
            op("V", "scalar_tensor_tensor", rt0T[tp_][par][:, :],
               rt_sl[tp_][:, :], mask2[:, par:par + 1], E_r0[tp_][:, :],
               ALU.mult, ALU.mult, chain=False)
            op("V", "scalar_tensor_tensor", rt1T[tp_][par][:, :],
               rt_sl[tp_][:, :].rearrange("p (c u) -> p c u", u=128)[:, :, 64:128],
               mask2[:, par:par + 1],
               E_r1[tp_][:, :].rearrange("p (c u) -> p c u", u=64),
               ALU.mult, ALU.mult, chain=False)
        ktv = None
        for half in range(2):
            ktv = op("V", "tensor_tensor",
                     ktlT[tp_][half][:, :].rearrange("p (c u) -> p c u", u=128)[:, :, half * 64:(half + 1) * 64],
                     kt_sl[tp_][:, :].rearrange("p (c u) -> p c u", u=128)[:, :, half * 64:(half + 1) * 64],
                     E_kt[tp_][:, :].rearrange("p (c u) -> p c u", u=128)[:, :, half * 64:(half + 1) * 64],
                     ALU.mult, chain=False, inc=(sV if half == 1 else None))
        slice_free_V[tt] = ktv
        ktv_tile[tt] = ktv
        load_slices(tt + 2)

    def stageT_a(tt):
        nonlocal last_khtpS
        tp_ = tt % 2
        wait("T", sV, ktv_tile[tt])
        wait("T", sV, stV_tile[tt - 1])
        for par in range(2):
            wait("T", sS, last_khtpS)
            last = None
            for dt in range(NDT):
                last = trans(pKtp[:, dt * 128:(dt + 1) * 128],
                             khT[tp_][par][:, dt * 128:(dt + 1) * 128],
                             identB[:, :], inc=(sT if dt == NDT - 1 else None))
            wait("S", sT, last)
            last_khtpS = op("S", "activation", khtp[par][:, :], pKtp[:, :],
                            AF.Copy, inc=sS)
        wait("T", sS, last_khtpS)
        wait("T", sV, last_cbv[0])
        last = None
        for h in range(8):
            dt = h // 2
            par = h % 2
            mm(pAtr[:, h * 128:(h + 1) * 128],
               ktlT[tp_][0][:, dt * 128:(dt + 1) * 128],
               rt0T[tp_][par][:, dt * 128:(dt + 1) * 128],
               True, False)
            last = mm(pAtr[:, h * 128 + 64:(h + 1) * 128],
                      ktlT[tp_][1][:, dt * 128:(dt + 1) * 128],
                      rt1T[tp_][par][:, dt * 64:(dt + 1) * 64],
                      False, True, inc=(sT if h == 7 else None))
        amm0_tile[tt] = last
        wait("T", sV, last_stV[0])
        last = None
        for dt in range(NDT):
            mm(pSup[:, dt * 64:(dt + 1) * 64],
               khtp[0][:, dt * 128:(dt + 1) * 128],
               v_tl[tp_][:, (2 * dt) * 64:(2 * dt + 1) * 64],
               True, False)
            last = mm(pSup[:, dt * 64:(dt + 1) * 64],
                      khtp[1][:, dt * 128:(dt + 1) * 128],
                      v_tl[tp_][:, (2 * dt + 1) * 64:(2 * dt + 2) * 64],
                      False, True, inc=(sT if dt == NDT - 1 else None))
        sup_tile[tt] = last
        tile_T[tt] = last

    def clampmask0(tt):
        wait("V", sT, amm0_tile[tt])
        cb0_tile[tt] = op("V", "tensor_scalar", clampb[:, 0:1024], pAtr[:, :],
                          1e30, -1e30, ALU.min, ALU.max, inc=sV)
        m0_tile[tt] = op("V", "tensor_tensor", Am[:, 0:1024], clampb[:, 0:1024],
                         maskA[:, 0:1024], ALU.mult, inc=sV)

    def stageT_b(tt):
        tp_ = tt % 2
        wait("T", sV, cb0_tile[tt])
        last = None
        for h in range(8, NH):
            dt = h // 2
            par = h % 2
            mm(pAtr[:, (h - 8) * 128:(h - 7) * 128],
               ktlT[tp_][0][:, dt * 128:(dt + 1) * 128],
               rt0T[tp_][par][:, dt * 128:(dt + 1) * 128],
               True, False)
            last = mm(pAtr[:, (h - 8) * 128 + 64:(h - 7) * 128],
                      ktlT[tp_][1][:, dt * 128:(dt + 1) * 128],
                      rt1T[tp_][par][:, dt * 64:(dt + 1) * 64],
                      False, True, inc=(sT if h == NH - 1 else None))
        amm1_tile[tt] = last

    def clampmask1(tt):
        wait("V", sT, amm1_tile[tt])
        last_cbv[0] = op("V", "tensor_scalar", clampb[:, 1024:2048], pAtr[:, :],
                         1e30, -1e30, ALU.min, ALU.max, inc=sV)
        m1_tile[tt] = op("V", "tensor_tensor", Am[:, 1024:2048],
                         clampb[:, 1024:2048], maskA[:, 1024:2048],
                         ALU.mult, inc=sV)

    def stageT_c(tt):
        nonlocal last_oev
        tp_ = tt % 2
        wait("T", sV, m0_tile[tt])
        wait("T", qVt[tp_], vt_y[tt])
        wait("T", sS, last_oev)
        for h in range(8):
            dt = h // 2
            par = h % 2
            mm(pOut[:, h * 64:(h + 1) * 64],
               Am[:, h * 128:(h + 1) * 128],
               v_tl[tp_][:, h * 64:(h + 1) * 64],
               True, tt == 0)
            if tt > 0:
                mm(pOut[:, h * 64:(h + 1) * 64],
                   rhT[tp_][par][:, dt * 128:(dt + 1) * 128],
                   state_bf[:, dt * 64:(dt + 1) * 64],
                   False, True)
        wait("T", sV, m1_tile[tt])
        av_last = None
        for h in range(8, NH):
            dt = h // 2
            par = h % 2
            is_last = (h == NH - 1)
            av_last = mm(pOut[:, h * 64:(h + 1) * 64],
                         Am[:, 1024 + (h - 8) * 128:1024 + (h - 7) * 128],
                         v_tl[tp_][:, h * 64:(h + 1) * 64],
                         True, tt == 0,
                         inc=(sT if (is_last and tt == 0) else None))
            if tt > 0:
                av_last = mm(pOut[:, h * 64:(h + 1) * 64],
                             rhT[tp_][par][:, dt * 128:(dt + 1) * 128],
                             state_bf[:, dt * 64:(dt + 1) * 64],
                             False, True, inc=(sT if is_last else None))
        av_T[tt] = av_last
        v_free_T[tt] = av_last
        wait("S", sT, av_last)
        if tt >= 2:
            wait("S", qOst[tp_], o_y[tt - 2])
        last_oev = op("S", "activation", o_tl[tp_][:, :], pOut[:, :], AF.Copy, inc=sS)
        wait("Y", sS, last_oev)
        o_y[tt] = dma("Y", qOst[tp_], o_d[:, tt * 1024:(tt + 1) * 1024], o_tl[tp_][:, :])

    def stateupd(tt):
        tp_ = tt % 2
        wait("V", sT, sup_tile[tt])
        wait("V", sS, scS_tile[tt])
        for dt in range(NDT):
            op("V", "scalar_tensor_tensor", state[:, dt * 64:(dt + 1) * 64],
               state[:, dt * 64:(dt + 1) * 64], edec[tp_][:, dt:dt + 1],
               pSup[:, dt * 64:(dt + 1) * 64], ALU.mult, ALU.add, chain=(dt == 0))
        stv = op("V", "tensor_copy", state_bf[:, :], state[:, :], inc=sV)
        last_stV[0] = stv
        stV_tile[tt] = stv

    def v_load(tt):
        tp_ = tt % 2
        if tt >= 2:
            wait("Y", sT, v_free_T[tt - 2])
        vt_y[tt] = dma("Y", qVt[tp_], v_tl[tp_][:, :], vsp_d[:, tt * 1024:(tt + 1) * 1024])

    NG = NT // 2
    load_slices(0)
    load_slices(1)
    for g in range(2):
        for dq in range(NDT // 2):
            emit_z_step(g, dq)
            emit_scans(g, dq)
    v_load(0)
    argsV(0)
    expsS(0)
    stageV2(0)
    pend = []
    for tt in range(NT):
        if tt + 1 < NT:
            v_load(tt + 1)
            argsV(tt + 1)
        for (gq, dq) in pend:
            emit_scans(gq, dq)
        pend = []
        stageT_a(tt)
        clampmask0(tt)
        stageT_b(tt)
        clampmask1(tt)
        if tt + 1 < NT:
            expsS(tt + 1)
        stageT_c(tt)
        stateupd(tt)
        if tt + 1 < NT:
            stageV2(tt + 1)
        g_n = tt // 2 + 2
        if g_n < NG:
            for dq in ((0, 1) if tt % 2 == 0 else (2, 3)):
                emit_z_step(g_n, dq)
                pend.append((g_n, dq))

    wait("T", sV, sV.n)
    wait("Y", qOst[0], qOst[0].n)
    wait("Y", qOst[1], qOst[1].n)
    dbg("o_d", o_d[:, :], BF16)
    dbg("Am15", Am[:, :], BF16)
    dbg("v15", v_tl[1][:, :], BF16)
    dbg("st15", state_bf[:, :], BF16)
    dbg("kh15", E_kh[1][:, :], BF16)
    dbg("kt15", kt_sl[1][:, :], BF16)
    dbg("vsp", vsp_d[:, :], BF16)
    a2.close()
    sW0.close()

    accT = E(nc.sbuf_tensor("accT", [128, NDT * T], F32))
    a3 = mkstack()
    oT = a3.enter_context(nc.sbuf_tensor("oT", [128, NDT * T], BF16, side="right"))
    Wo_sl = a3.enter_context(nc.sbuf_tensor("Wo_sl", [128, NDT * 1024], BF16, side="right"))
    o_tl3 = [a3.enter_context(nc.sbuf_tensor(f"o_tl3{i}", [128, 1024], BF16, side="right"))
             for i in range(2)]
    a3t = mkstack()
    pT3 = [a3t.enter_context(nc.psum_tensor(f"pT3{i}", [128, 1024], BF16)) for i in range(2)]

    barrier("G")
    barrier("Y")
    wait("Y", qOst[0], qOst[0].n)
    wait("Y", qOst[1], qOst[1].n)
    wo_ld = load_w(Wo_d, Wo_sl, qWoS, twait=0)
    otr = {}
    oy3 = {}
    otr_T = {}
    for tt in range(NT):
        pp = pT3[tt % 2]
        if tt >= 2:
            wait("Y", sT, otr_T[tt - 2])
        oy3[tt] = dma("Y", qOl3[tt % 2], o_tl3[tt % 2][:, :], o_d[:, tt * 1024:(tt + 1) * 1024])
        if tt >= 2:
            wait("T", sV, otr[tt - 2])
        wait("T", qOl3[tt % 2], oy3[tt])
        last = None
        for c in range(NDT):
            last = trans(pp[:, c * 128:(c + 1) * 128],
                         o_tl3[tt % 2][:, c * 128:(c + 1) * 128],
                         identB[:, :], inc=(sT if c == NDT - 1 else None))
        otr_T[tt] = last
        wait("V", sT, last)
        otr[tt] = op("V", "tensor_copy",
                     oT[:, :].rearrange("p (c u) -> p c u", u=T)[:, :, tt * 128:(tt + 1) * 128],
                     pp[:, :].rearrange("p (c u) -> p c u", u=128), inc=sV)

    a3t.close()
    a3p = mkstack()
    pW3 = [a3p.enter_context(nc.psum_tensor(f"pW3{i}", [128, 2048], F32)) for i in range(2)]
    wait("T", qWoS, wo_ld)
    wait("T", sV, sV.n)
    wo_ev = {}
    for dt in range(NDT):
        pp = pW3[dt % 2]
        if dt >= 2:
            wait("T", sV, wo_ev[dt - 2])
        last = None
        for ct in range(NDT):
            lhs = Wo_sl[:, ct * 1024 + dt * 128: ct * 1024 + dt * 128 + 128]
            for ts in range(4):
                last = mm(pp[:, ts * 512:(ts + 1) * 512], lhs,
                          oT[:, ct * T + ts * 512: ct * T + (ts + 1) * 512],
                          ct == 0, ct == NDT - 1,
                          inc=(sT if (ct == NDT - 1 and ts == 3) else None))
        wait("V", sT, last)
        wo_ev[dt] = op("V", "tensor_copy", accT[:, dt * T:(dt + 1) * T], pp[:, :], inc=sV)

    dbg("accT_attn", accT[:, :], F32)
    wait("T", sV, sV.n)
    a3.close()
    a3p.close()

    if "b" not in phases:
        return _finish(nc, all_stacks, L, sO, dbg_info)

    b_ = mkstack()
    EB = b_.enter_context
    hTs = EB(nc.sbuf_tensor("hTs", [128, EBLK * NFB * 1024], BF16, side="right"))
    Wup_sl = [EB(nc.sbuf_tensor(f"Wup{i}", [128, NDT * 512], BF16, side="right")) for i in range(EBLK)]
    Wdn_sl = [EB(nc.sbuf_tensor(f"Wdn{i}", [128, NFB * 1024], BF16, side="right")) for i in range(EBLK)]
    cwbc = EB(nc.sbuf_tensor("cwbc", [128, EBLK * T], BF16, side="right"))
    pUp = [EB(nc.psum_tensor(f"pUp{i}", [128, 1024], F32)) for i in range(2)]
    pDn = [EB(nc.psum_tensor(f"pDn{i}", [128, 1024], F32)) for i in range(2)]

    experts = [-1] + list(range(NE))
    blocks = [experts[i:i + EBLK] for i in range(0, len(experts), EBLK)]

    def wup_load(e, slot, twait=0):
        wait("G", sT, twait)
        src = Wsu_d if e < 0 else Wup_d[e]
        return dma("G", qUp[slot], Wup_sl[slot][:, :].rearrange("p (c u) -> p c u", u=512),
                   src.rearrange("(c p) u -> p c u", p=128))

    def wdn_load(e, slot, twait=0):
        wait("G", sT, twait)
        src = Wsd_d if e < 0 else Wdn_d[e]
        return dma("G", qDn[slot], Wdn_sl[slot][:, :].rearrange("p (c u) -> p c u", u=1024),
                   src.rearrange("(c p) u -> p c u", p=128))

    up_ld = {}
    dn_ld = {}
    bc_ld = {}
    up_slot_T = [0] * EBLK
    bc_war = [0] * EBLK
    last_pUpS = [0, 0]
    last_pUpV = [0, 0]
    last_pDnV = [0, 0]
    last_dn_T = 0
    upr = 0
    h_done = {}

    barrier("G")
    barrier("Y")
    for ei, e in enumerate(blocks[0]):
        up_ld[(0, ei)] = wup_load(e, ei)
        dn_ld[(0, ei)] = wdn_load(e, ei)

    for bi, blk_ in enumerate(blocks):
        nb = len(blk_)
        for ei, e in enumerate(blk_):
            if e < 0:
                op("V", "memset", cwbc[:, ei * T:(ei + 1) * T], 1.0, inc=sV)
                bc_ld[(bi, ei)] = 0
                continue
            wait("G", sV, bc_war[ei])
            if bi == 0:
                wait("G", qRow, cw_store)
            bc_ld[(bi, ei)] = dma("G", qBc[ei], cwbc[:, ei * T:(ei + 1) * T],
                                  cwT_d[e:e + 1, :].partition_broadcast(128))
        for th in range(2):
            for ei, e in enumerate(blk_):
                wait("T", qUp[ei], up_ld[(bi, ei)])
                for fb in range(NFB):
                    reg = upr % 2
                    upr += 1
                    pp = pUp[reg]
                    wait("T", sS, last_pUpS[reg])
                    wait("T", sV, last_pUpV[reg])
                    last = None
                    for ct in range(NDT):
                        lhs = Wup_sl[ei][:, ct * 512 + fb * 128: ct * 512 + fb * 128 + 128]
                        for ts in range(2):
                            last = mm(pp[:, ts * 512:(ts + 1) * 512], lhs,
                                      xnT[:, ct * T + th * 1024 + ts * 512:
                                          ct * T + th * 1024 + (ts + 1) * 512],
                                      ct == 0, ct == NDT - 1,
                                      inc=(sT if (ct == NDT - 1 and ts == 1) else None))
                    up_slot_T[ei] = last
                    h_sl = hTs[:, ei * 4096 + fb * 1024: ei * 4096 + (fb + 1) * 1024]
                    wait("S", sT, last)
                    wait("S", sT, last_dn_T)
                    sS_ = op("S", "activation", h_sl, pp[:, :], AF.Silu, inc=sS)
                    last_pUpS[reg] = sS_
                    wait("V", sS, sS_)
                    if bc_ld[(bi, ei)]:
                        wait("V", qBc[ei], bc_ld[(bi, ei)])
                    hv = op("V", "tensor_tensor", h_sl, h_sl,
                            cwbc[:, ei * T + th * 1024: ei * T + (th + 1) * 1024],
                            ALU.mult, inc=sV)
                    bc_war[ei] = hv
                    h_done[(ei, fb)] = hv
            for ei in range(nb):
                wait("T", qDn[ei], dn_ld[(bi, ei)])
            for dt in range(NDT):
                reg = dt % 2
                pp = pDn[reg]
                wait("T", sV, last_pDnV[reg])
                last = None
                for ei in range(nb):
                    for fb in range(NFB):
                        if dt == 0:
                            wait("T", sV, h_done[(ei, fb)])
                        lhs = Wdn_sl[ei][:, fb * 1024 + dt * 128: fb * 1024 + dt * 128 + 128]
                        for ts in range(2):
                            stop = (ei == nb - 1 and fb == NFB - 1)
                            last = mm(pp[:, ts * 512:(ts + 1) * 512], lhs,
                                      hTs[:, ei * 4096 + fb * 1024 + ts * 512:
                                          ei * 4096 + fb * 1024 + (ts + 1) * 512],
                                      (ei == 0 and fb == 0), stop,
                                      inc=(sT if (stop and ts == 1) else None))
                wait("V", sT, last)
                acc_sl = accT[:, dt * T + th * 1024: dt * T + (th + 1) * 1024]
                last_pDnV[reg] = op("V", "tensor_tensor", acc_sl, acc_sl, pp[:, :],
                                    ALU.add, inc=sV)
            last_dn_T = sT.n
        if bi + 1 < len(blocks):
            for ei, e in enumerate(blocks[bi + 1]):
                up_ld[(bi + 1, ei)] = wup_load(e, ei, twait=up_slot_T[ei])
                dn_ld[(bi + 1, ei)] = wdn_load(e, ei, twait=last_dn_T)

    dbg("accT", accT[:, :], F32)
    wait("T", sV, sV.n)
    b_.close()

    if "c" not in phases:
        return _finish(nc, all_stacks, L, sO, dbg_info)

    c_ = mkstack()
    EC = c_.enter_context
    xcb = [EC(nc.sbuf_tensor(f"xcb{i}", [128, D], F32, side="right")) for i in range(2)]
    outb = [EC(nc.sbuf_tensor(f"outb{i}", [128, D], F32, side="right")) for i in range(2)]
    pC = [EC(nc.psum_tensor(f"pC{i}", [128, 1024], F32)) for i in range(2)]

    barrier("Y")
    ctr_ev = {}
    x_y = {}
    out_y = {}

    def c_load(tt):
        if tt >= NT:
            return
        if tt >= 2:
            wait("Y", sV, ctr_ev[tt - 2])
        x_y[tt] = dma("Y", qX[tt % 2], xcb[tt % 2][:, :], xin[tt * 128:(tt + 1) * 128, :])

    c_load(0)
    for tt in range(NT):
        pp = pC[tt % 2]
        c_load(tt + 1)
        if tt >= 2:
            wait("T", sV, ctr_ev[tt - 2])
        last = None
        for dt in range(NDT):
            last = trans(pp[:, dt * 128:(dt + 1) * 128],
                         accT[:, dt * T + tt * 128: dt * T + (tt + 1) * 128],
                         identF[:, :], inc=(sT if dt == NDT - 1 else None))
        wait("V", sT, last)
        wait("V", qX[tt % 2], x_y[tt])
        if tt >= 2:
            wait("V", qOut[tt % 2], out_y[tt - 2])
        ctr_ev[tt] = op("V", "tensor_tensor", outb[tt % 2][:, :], pp[:, 0:D],
                        xcb[tt % 2][:, :], ALU.add, inc=sV)
        wait("Y", sV, ctr_ev[tt])
        out_y[tt] = dma("Y", qOut[tt % 2], out_d[tt * 128:(tt + 1) * 128, :], outb[tt % 2][:, :])
    wait("Y", qOut[0], qOut[0].n)
    wait("Y", qOut[1], qOut[1].n)
    c_.close()

    return _finish(nc, all_stacks, L, sO, dbg_info)


def _finish(nc, all_stacks, L, sO, dbg_info):
    L["Y"].append(lambda g, h=sO.h, n=sO.n: g.wait_ge(h, n))
    with nc.Block() as blk:
        @blk.tensor
        def _(t):
            for f in L["T"]:
                f(t)

        @blk.vector
        def _(v):
            for f in L["V"]:
                f(v)

        @blk.scalar
        def _(s):
            for f in L["S"]:
                f(s)

        @blk.gpsimd
        def _(g):
            for f in L["G"]:
                f(g)

        @blk.sync
        def _(y):
            for f in L["Y"]:
                f(y)
    for s in reversed(all_stacks):
        s.close()
    return nc, dbg_info


def host_consts():
    identF = np.eye(128, dtype=np.float32)
    identB = identF.astype(ml_dtypes.bfloat16)
    tri = (np.arange(128)[None, :] >= np.arange(128)[:, None]).astype(np.float32)
    maskA = np.tile(tri, (1, 16)).astype(ml_dtypes.bfloat16)
    return identB, identF, maskA


def make_inmaps(L0_attn_x, L0_attn_ln_g, L0_attn_ln_b, L0_attn_Wx, L0_attn_Ww,
                L0_attn_bw, L0_attn_Wk, L0_attn_Wv, L0_attn_Wr, L0_attn_Wo,
                L0_ffn_centroids, L0_ffn_expert_biases, L0_ffn_Ws_up,
                L0_ffn_Ws_down, L0_ffn_W_up, L0_ffn_W_down):
    identB, identF, maskA = host_consts()
    f32 = lambda a: np.ascontiguousarray(np.asarray(a, dtype=np.float32))
    shared = dict(
        ln_g=f32(L0_attn_ln_g).reshape(1, D),
        ln_b=f32(L0_attn_ln_b).reshape(1, D),
        Wx=f32(L0_attn_Wx), Ww=f32(L0_attn_Ww),
        bw=np.ascontiguousarray(f32(L0_attn_bw).reshape(NDT, 128).T),
        Wk=f32(L0_attn_Wk), Wv=f32(L0_attn_Wv), Wr=f32(L0_attn_Wr), Wo=f32(L0_attn_Wo),
        cent=f32(L0_ffn_centroids),
        ebias=f32(L0_ffn_expert_biases).reshape(1, NE),
        Wsu=f32(L0_ffn_Ws_up), Wsd=f32(L0_ffn_Ws_down),
        Wup=f32(L0_ffn_W_up), Wdn=f32(L0_ffn_W_down),
        identB=identB, identF=identF, maskA=maskA,
    )
    xs = np.asarray(L0_attn_x, dtype=np.float32)
    return [dict(shared, x=np.ascontiguousarray(xs[i])) for i in range(xs.shape[0])]


def kernel(**inputs):
    nc, _ = build()
    in_maps = make_inmaps(**inputs)
    res = run_bass_kernel_spmd(nc, in_maps, core_ids=list(range(8)))
    return np.stack([np.asarray(res.results[i]["out"]) for i in range(8)], axis=0)


# version 18
# speedup vs baseline: 1.0851x; 1.0851x over previous
import numpy as np
import ml_dtypes
from contextlib import ExitStack

import concourse.bass as bass
import concourse.mybir as mybir
from concourse.bass_utils import run_bass_kernel_spmd

F32 = mybir.dt.float32
BF16 = mybir.dt.bfloat16
AF = mybir.ActivationFunctionType
ALU = mybir.AluOpType
AX = mybir.AxisListType

T = 2048
D = 1024
NH = 16
FF = 512
NE = 64
NT = T // 128
NDT = D // 128
NFB = FF // 128
EPS = 1e-5
EBLK = 3


class Sem:
    def __init__(self, ctx, nc, name):
        self.h = ctx.enter_context(nc.semaphore(name))
        self.n = 0


def build(dbg_names=(), phases="0123bc", a2_groups=NT // 2, a2_tiles=2, a2_sub=9):
    nc = bass.Bass("TRN2", target_bir_lowering=False)

    xin = nc.dram_tensor("x", [T, D], F32, kind="ExternalInput").ap()
    ln_g = nc.dram_tensor("ln_g", [1, D], F32, kind="ExternalInput").ap()
    ln_b = nc.dram_tensor("ln_b", [1, D], F32, kind="ExternalInput").ap()
    Wx_d = nc.dram_tensor("Wx", [D, D], F32, kind="ExternalInput").ap()
    Ww_d = nc.dram_tensor("Ww", [D, D], F32, kind="ExternalInput").ap()
    bw_d = nc.dram_tensor("bw", [128, NDT], F32, kind="ExternalInput").ap()
    Wk_d = nc.dram_tensor("Wk", [D, D], F32, kind="ExternalInput").ap()
    Wv_d = nc.dram_tensor("Wv", [D, D], F32, kind="ExternalInput").ap()
    Wr_d = nc.dram_tensor("Wr", [D, D], F32, kind="ExternalInput").ap()
    Wo_d = nc.dram_tensor("Wo", [D, D], F32, kind="ExternalInput").ap()
    cent_d = nc.dram_tensor("cent", [NE, D], F32, kind="ExternalInput").ap()
    ebias_d = nc.dram_tensor("ebias", [1, NE], F32, kind="ExternalInput").ap()
    Wsu_d = nc.dram_tensor("Wsu", [D, FF], F32, kind="ExternalInput").ap()
    Wsd_d = nc.dram_tensor("Wsd", [FF, D], F32, kind="ExternalInput").ap()
    Wup_d = nc.dram_tensor("Wup", [NE, D, FF], F32, kind="ExternalInput").ap()
    Wdn_d = nc.dram_tensor("Wdn", [NE, FF, D], F32, kind="ExternalInput").ap()
    identB_d = nc.dram_tensor("identB", [128, 128], BF16, kind="ExternalInput").ap()
    identF_d = nc.dram_tensor("identF", [128, 128], F32, kind="ExternalInput").ap()
    maskA_d = nc.dram_tensor("maskA", [128, 2048], BF16, kind="ExternalInput").ap()
    out_d = nc.dram_tensor("out", [T, D], F32, kind="ExternalOutput").ap()
    t1T_d = nc.dram_tensor("t1T_scr", [128, NDT * T], BF16, kind="Internal").ap()
    vsp_d = nc.dram_tensor("v_scr", [128, NT * D], BF16, kind="Internal").ap()
    o_d = nc.dram_tensor("o_scr", [128, NT * D], BF16, kind="Internal").ap()
    cwT_d = nc.dram_tensor("cwT_scr", [NE, T], F32, kind="Internal").ap()
    kT_d = nc.dram_tensor("kT_scr", [128, NDT * T], BF16, kind="Internal").ap()
    rT_d = nc.dram_tensor("rT_scr", [128, NDT * T], BF16, kind="Internal").ap()

    TL, VL, SL, GL, YL = [], [], [], [], []
    L = {"T": TL, "V": VL, "S": SL, "G": GL, "Y": YL}

    stack = ExitStack()
    E = stack.enter_context
    all_stacks = [stack]
    def mkstack():
        s = ExitStack()
        all_stacks.append(s)
        return s

    sT = Sem(stack, nc, "sT")
    sV = Sem(stack, nc, "sV")
    sS = Sem(stack, nc, "sS")
    sG = Sem(stack, nc, "sG")
    sO = Sem(stack, nc, "sO")
    qCst = Sem(stack, nc, "qCst")
    qRow = Sem(stack, nc, "qRow")
    qX = [Sem(stack, nc, f"qX{i}") for i in range(2)]
    qSp = [Sem(stack, nc, f"qSp{i}") for i in range(2)]
    qSv = [Sem(stack, nc, f"qSv{i}") for i in range(2)]
    qT1g = [Sem(stack, nc, f"qT1g{i}") for i in range(2)]
    qVt = [Sem(stack, nc, f"qVt{i}") for i in range(2)]
    qOst = [Sem(stack, nc, f"qOst{i}") for i in range(2)]
    qOl3 = [Sem(stack, nc, f"qOl3{i}") for i in range(2)]
    qW0 = Sem(stack, nc, "qW0")
    qW1 = Sem(stack, nc, "qW1")
    qWoS = Sem(stack, nc, "qWoS")
    qUp = [Sem(stack, nc, f"qUp{i}") for i in range(EBLK)]
    qDn = [Sem(stack, nc, f"qDn{i}") for i in range(EBLK)]
    qCw = Sem(stack, nc, "qCw")
    qBc = [Sem(stack, nc, f"qBc{i}") for i in range(EBLK)]
    qKt = [Sem(stack, nc, f"qKt{i}") for i in range(2)]
    qRt = [Sem(stack, nc, f"qRt{i}") for i in range(2)]
    qOut = [Sem(stack, nc, f"qOut{i}") for i in range(2)]

    def wait(e, sem, n):
        if n > 0:
            L[e].append(lambda g, h=sem.h, n=n: g.wait_ge(h, n))

    def barrier(e):
        wait(e, sT, sT.n)
        wait(e, sV, sV.n)
        wait(e, sS, sS.n)

    def dma(e, sem, out_ap, in_ap):
        sem.n += 16
        L[e].append(lambda g, o=out_ap, i=in_ap, h=sem.h: g.dma_start(o, i).then_inc(h, 16))
        return sem.n

    def mm(out_ap, lhs_ap, rhs_ap, start, stop, tp=None, inc=None):
        if inc is not None:
            inc.n += 1
        def f(t, o=out_ap, l=lhs_ap, r=rhs_ap, s=start, p=stop, tp=tp, inc=inc):
            kw = {} if tp is None else {"tile_position": tp}
            ins = t.matmul(o, l, r, start=s, stop=p, **kw)
            if inc is not None:
                ins.then_inc(inc.h, 1)
        TL.append(f)
        return None if inc is None else inc.n

    def trans(out_ap, in_ap, ident_ap, inc=None):
        if inc is not None:
            inc.n += 1
        def f(t, o=out_ap, i=in_ap, d=ident_ap, inc=inc):
            ins = t.transpose(o, i, d)
            if inc is not None:
                ins.then_inc(inc.h, 1)
        TL.append(f)
        return None if inc is None else inc.n

    def op(e, name, *args, inc=None, chain=True, **kwargs):
        if e == "V":
            if chain and sV.n > 0:
                L["V"].append(lambda g, h=sV.h, n=sV.n: g.wait_ge(h, n))
            inc = sV
        if e == "G" and name not in ("dma_start",):
            if sG.n > 0:
                L["G"].append(lambda g, h=sG.h, n=sG.n: g.wait_ge(h, n))
            inc = sG
        if inc is not None:
            inc.n += 1
        def f(g, name=name, args=args, kwargs=kwargs, inc=inc):
            ins = getattr(g, name)(*args, **kwargs)
            if inc is not None:
                ins.then_inc(inc.h, 1)
        L[e].append(f)
        return None if inc is None else inc.n

    dbg_info = {}

    def dbg(name, ap_, dt):
        if name in dbg_names:
            shp = [int(s) for s in ap_.shape]
            o = nc.dram_tensor("dbg_" + name, shp, dt, kind="ExternalOutput").ap()
            dbg_info[name] = dt
            wait("Y", sV, sV.n)
            wait("Y", sS, sS.n)
            wait("Y", sT, sT.n)
            dma("Y", sO, o, ap_)

    xnT = E(nc.sbuf_tensor("xnT", [128, NDT * T], BF16))
    cwT = E(nc.sbuf_tensor("cwT", [NE, T], F32))
    identB = E(nc.sbuf_tensor("identB_s", [128, 128], BF16))
    identF = E(nc.sbuf_tensor("identF_s", [128, 128], F32))
    maskA = E(nc.sbuf_tensor("maskA_s", [128, 2048], BF16))
    bw_sb = E(nc.sbuf_tensor("bw_sb", [128, NDT], F32))
    ones1 = E(nc.sbuf_tensor("ones1", [1, 128], F32))
    zeros128 = E(nc.sbuf_tensor("zeros128", [128, 128], F32))
    mask2 = E(nc.sbuf_tensor("mask2", [128, 2], F32))

    dma("G", qCst, identB[:, :], identB_d)
    dma("G", qCst, identF[:, :], identF_d)
    dma("G", qCst, maskA[:, :], maskA_d)
    dma("G", qCst, bw_sb[:, :], bw_d)
    op("G", "memset", ones1[:, :], 1.0)
    op("G", "memset", zeros128[:, :], 0.0)
    op("G", "memset", mask2[0:64, 0:1], 1.0)
    op("G", "memset", mask2[64:128, 0:1], 0.0)
    op("G", "memset", mask2[0:64, 1:2], 0.0)
    op("G", "memset", mask2[64:128, 1:2], 1.0)
    G_consts = qCst.n
    wait("T", sG, sG.n)
    wait("V", sG, sG.n)
    wait("S", sG, sG.n)

    p0 = mkstack()
    E0r = p0.enter_context
    E0 = lambda cm: E0r(cm)
    s01 = mkstack()
    xn2T = s01.enter_context(nc.sbuf_tensor("xn2T", [128, NDT * T], BF16, side="right"))
    xnloT = E0(nc.sbuf_tensor("xnloT", [128, NDT * T], BF16, side="right"))
    g_bc = E0(nc.sbuf_tensor("g_bc", [128, D], F32, side="right"))
    b_bc = E0(nc.sbuf_tensor("b_bc", [128, D], F32, side="right"))
    eb_bc = E0(nc.sbuf_tensor("eb_bc", [128, NE], F32, side="right"))
    grow = E0(nc.sbuf_tensor("grow", [1, D], F32, side="right"))
    brow = E0(nc.sbuf_tensor("brow", [1, D], F32, side="right"))
    erow = E0(nc.sbuf_tensor("erow", [1, NE], F32, side="right"))
    cent_sb = E0(nc.sbuf_tensor("cent_sb", [NE, D], F32, side="right"))
    cent_hi = E0(nc.sbuf_tensor("cent_hi", [NE, D], BF16, side="right"))
    cent_lo = E0(nc.sbuf_tensor("cent_lo", [NE, D], BF16, side="right"))
    centT_hi = E0(nc.sbuf_tensor("centT_hi", [128, NDT * NE], BF16, side="right"))
    centT_lo = E0(nc.sbuf_tensor("centT_lo", [128, NDT * NE], BF16, side="right"))
    affT_sb = E0(nc.sbuf_tensor("affT_sb", [64, T], F32, side="right"))
    cols = [E0(nc.sbuf_tensor(f"cols{i}", [128, 16], F32, side="right")) for i in range(2)]
    aff_sb = E0(nc.sbuf_tensor("aff_sb", [128, NT * NE], F32, side="right"))
    bias_t = E0(nc.sbuf_tensor("bias_t", [128, NE], F32, side="right"))
    m8 = E0(nc.sbuf_tensor("m8", [128, 64], F32, side="right"))
    sgm = E0(nc.sbuf_tensor("sgm", [128, NE], F32, side="right"))
    wsel = E0(nc.sbuf_tensor("wsel", [128, NE], F32, side="right"))
    cw_t = E0(nc.sbuf_tensor("cw_t", [128, NE], F32, side="right"))

    p0ln = mkstack()
    Eln = p0ln.enter_context
    xtl = [Eln(nc.sbuf_tensor(f"xtl{i}", [128, D], F32, side="right")) for i in range(2)]
    lnt = [Eln(nc.sbuf_tensor(f"lnt{i}", [128, D], F32, side="right")) for i in range(2)]
    ln2t = [Eln(nc.sbuf_tensor(f"ln2t{i}", [128, D], F32, side="right")) for i in range(2)]
    lnb = [Eln(nc.sbuf_tensor(f"lnb{i}", [128, D], BF16, side="right")) for i in range(2)]
    lnlob = [Eln(nc.sbuf_tensor(f"lnlob{i}", [128, D], BF16, side="right")) for i in range(2)]
    ln2b = [Eln(nc.sbuf_tensor(f"ln2b{i}", [128, D], BF16, side="right")) for i in range(2)]

    pT0 = [E0(nc.psum_tensor(f"pT0{i}", [128, 1024], BF16)) for i in range(2)]
    pP0f = E0(nc.psum_tensor("pP0f", [128, 1024], F32))
    pA_st = mkstack()
    pAff = pA_st.enter_context(nc.psum_tensor("pAff", [64, T], F32))

    dma("Y", qRow, grow[:, :], ln_g)
    dma("Y", qRow, brow[:, :], ln_b)
    dma("Y", qRow, erow[:, :], ebias_d)
    rows_y = dma("Y", qRow, cent_sb[:, :], cent_d)
    wait("T", qRow, rows_y)
    wait("T", qCst, G_consts)
    for (dst, src, n) in ((g_bc, grow, D), (b_bc, brow, D), (eb_bc, erow, NE)):
        last = None
        for j in range(0, n, 512):
            w_ = min(512, n - j)
            last = mm(pP0f[:, j:j + w_], ones1[:, :], src[:, j:j + w_], True, True,
                      inc=(sT if j + 512 >= n else None))
        wait("V", sT, last)
        ev = op("V", "tensor_copy", dst[:, 0:n], pP0f[:, 0:n], inc=sV)
        wait("T", sV, ev)

    wait("V", qRow, rows_y)
    op("V", "tensor_copy", cent_hi[:, :], cent_sb[:, :])
    chl = op("V", "tensor_tensor", cent_lo[:, :], cent_sb[:, :], cent_hi[:, :],
             ALU.subtract, inc=sV)
    wait("T", sV, chl)
    for (src, dstT) in ((cent_hi, centT_hi), (cent_lo, centT_lo)):
        last = None
        for ct in range(NDT):
            last = trans(pT0[0][:, ct * 64:(ct + 1) * 64], src[0:64, ct * 128:(ct + 1) * 128],
                         identB[0:64, 0:64], inc=(sT if ct == NDT - 1 else None))
        wait("V", sT, last)
        ev = op("V", "tensor_copy", dstT[:, :], pT0[0][:, 0:NDT * 64], inc=sV)
        wait("T", sV, ev)

    sW0 = mkstack()
    Wsl0 = sW0.enter_context(nc.sbuf_tensor("Wsl0", [128, NDT * 1024], BF16))

    def load_w(dram_ap, slot_t, qsem, twait=0):
        wait("G", sT, twait)
        return dma("G", qsem, slot_t[:, :].rearrange("p (c u) -> p c u", u=1024),
                   dram_ap.rearrange("(c p) u -> p c u", p=128))

    wx_ld = load_w(Wx_d, Wsl0, qW0)

    wait("S", qCst, G_consts)
    rD = 1.0 / D
    ln_stats_S = {}
    ln_stats_V = {}
    split_v = {}
    tr_ev = {}
    tr_T_last = {-1: 0, -2: 0}
    x_ln_y = {}
    g3 = 0

    def ln_A(tt):
        i2 = tt % 2
        xb, lt, cl = xtl[i2], lnt[i2], cols[i2]
        if tt >= 2:
            wait("Y", sV, split_v[tt - 2])
        x_ln_y[tt] = dma("Y", qX[i2], xb[:, :], xin[tt * 128:(tt + 1) * 128, :])
        wait("S", qX[i2], x_ln_y[tt])
        if tt >= 2:
            wait("S", sV, split_v[tt - 2])
        sq = op("S", "activation", ln2t[i2][:, :], xb[:, :], AF.Square, inc=sS)
        wait("V", qX[i2], x_ln_y[tt])
        op("V", "reduce_sum", cl[:, 0:1], xb[:, :], AX.X)
        wait("V", sS, sq)
        op("V", "reduce_sum", cl[:, 1:2], ln2t[i2][:, :], AX.X)
        op("V", "tensor_scalar", cl[:, 2:3], cl[:, 0:1], rD, None, ALU.mult)
        op("V", "tensor_scalar", cl[:, 3:4], cl[:, 1:2], rD, None, ALU.mult)
        op("V", "tensor_scalar", cl[:, 4:5], cl[:, 2:3], cl[:, 2:3], None, ALU.mult)
        op("V", "tensor_tensor", cl[:, 5:6], cl[:, 3:4], cl[:, 4:5], ALU.subtract)
        vv = op("V", "tensor_scalar", cl[:, 6:7], cl[:, 5:6], EPS, None, ALU.add, inc=sV)
        wait("S", sV, vv)
        op("S", "activation", cl[:, 8:9], cl[:, 6:7], AF.Ln)
        s2 = op("S", "activation", cl[:, 9:10], cl[:, 8:9], AF.Exp, scale=-0.5, inc=sS)
        ln_stats_S[tt] = s2

    def ln_B(tt):
        nonlocal g3
        i2 = tt % 2
        xb, lt, l2t, lb, llo, l2b, cl = (xtl[i2], lnt[i2], ln2t[i2], lnb[i2],
                                         lnlob[i2], ln2b[i2], cols[i2])
        wait("V", sS, ln_stats_S[tt])
        op("V", "reciprocal", cl[:, 7:8], cl[:, 6:7])
        vv = op("V", "tensor_scalar", cl[:, 10:11], cl[:, 5:6], cl[:, 7:8], EPS,
                ALU.mult, ALU.add, inc=sV)
        wait("S", sV, vv)
        op("S", "activation", cl[:, 12:13], cl[:, 10:11], AF.Ln)
        s2 = op("S", "activation", cl[:, 13:14], cl[:, 12:13], AF.Exp, scale=-0.5, inc=sS)
        wait("V", sS, s2)
        op("V", "tensor_scalar", cl[:, 14:15], cl[:, 9:10], cl[:, 13:14], None, ALU.mult)
        op("V", "tensor_scalar", cl[:, 8:9], cl[:, 2:3], cl[:, 9:10], -1.0, ALU.mult, ALU.mult)
        op("V", "tensor_scalar", cl[:, 15:16], cl[:, 2:3], cl[:, 14:15], -1.0,
           ALU.mult, ALU.mult)
        wait("V", sT, tr_T_last[tt - 2])
        op("V", "tensor_scalar", lt[:, :], xb[:, :], cl[:, 9:10], cl[:, 8:9],
           ALU.mult, ALU.add)
        op("V", "tensor_scalar", l2t[:, :], xb[:, :], cl[:, 14:15], cl[:, 15:16],
           ALU.mult, ALU.add)
        op("V", "tensor_copy", lb[:, :], lt[:, :])
        op("V", "tensor_tensor", llo[:, :], lt[:, :], lb[:, :], ALU.subtract)
        op("V", "tensor_tensor", l2t[:, :], l2t[:, :], g_bc[:, :], ALU.mult)
        split_v[tt] = op("V", "tensor_tensor", l2b[:, :], l2t[:, :], b_bc[:, :],
                         ALU.add, inc=sV)
        wait("T", sV, split_v[tt])
        for (src_, dstT) in ((lb, xnT), (llo, xnloT), (l2b, xn2T)):
            pp = pT0[g3 % 2]
            if g3 >= 2:
                wait("T", sS, tr_ev[g3 - 2])
            last = None
            for c in range(NDT):
                last = trans(pp[:, c * 128:(c + 1) * 128], src_[:, c * 128:(c + 1) * 128],
                             identB[:, :], inc=(sT if c == NDT - 1 else None))
            tr_T_last[tt] = last
            wait("S", sT, last)
            o_ap = dstT[:, :].rearrange("p (c u) -> p c u", u=T)[:, :, tt * 128:(tt + 1) * 128]
            i_ap = pp[:, :].rearrange("p (c u) -> p c u", u=128)
            tr_ev[g3] = op("S", "activation", o_ap, i_ap, AF.Copy, inc=sS)
            g3 += 1

    ln_A(0)
    for tt in range(NT):
        if tt + 1 < NT:
            ln_A(tt + 1)
        ln_B(tt)

    dbg("xnT", xnT[:, :], BF16)
    dbg("xn2T", xn2T[:, :], BF16)
    p0ln.close()
    sW1 = mkstack()
    Wsl1 = sW1.enter_context(nc.sbuf_tensor("Wsl1", [128, NDT * 1024], BF16))
    sSPL = mkstack()
    spl = [sSPL.enter_context(nc.sbuf_tensor(f"spl{i}", [128, 2048], BF16)) for i in range(2)]
    wait("G", sV, sV.n)
    wait("G", sS, sS.n)
    wait("G", qX[0], qX[0].n)
    wait("G", qX[1], qX[1].n)
    wk_ld = load_w(Wk_d, Wsl1, qW1)

    wait("T", sV, sV.n)
    wait("T", sS, sS.n)
    aff_last = None
    for ts in range(4):
        first = True
        for (cT, xT) in ((centT_hi, xnT), (centT_hi, xnloT), (centT_lo, xnT)):
            for ct in range(NDT):
                stop = (cT is centT_lo) and (ct == NDT - 1)
                aff_last = mm(pAff[:, ts * 512:(ts + 1) * 512],
                              cT[:, ct * 64:(ct + 1) * 64],
                              xT[:, ct * T + ts * 512: ct * T + (ts + 1) * 512],
                              first, stop,
                              inc=(sT if (stop and ts == 3) else None))
                first = False
    wait("V", sT, aff_last)
    aff_ev = op("V", "tensor_copy", affT_sb[:, :], pAff[:, :], inc=sV)

    pA_st.close()
    a1p = mkstack()
    pWs = [a1p.enter_context(nc.psum_tensor(f"pW{i}", [128, 512], F32)) for i in range(2)]
    pj_ev = {-1: (sV, 0), -2: (sV, 0)}
    pj_y = {0: 0, 1: 0}
    vj_y = {0: 0, 1: 0}
    pj_r = [0]

    def proj_dpart(slot_t, w_ready, out_dram, sigmoid=False):
        wait("T", w_ready[0], w_ready[1])
        wait("T", sV, sV.n)
        wait("T", sS, sS.n)
        eng = "S" if sigmoid else "V"
        sem = sS if sigmoid else sV
        for dt in range(NDT):
            sb = spl[dt % 2]
            for ts in range(4):
                r = pj_r[0]
                pj_r[0] += 1
                pp = pWs[r % 2]
                wait("T", *pj_ev[r - 2])
                last = None
                for ct in range(NDT):
                    lhs = slot_t[:, ct * 1024 + dt * 128: ct * 1024 + dt * 128 + 128]
                    last = mm(pp[:, :], lhs,
                              xn2T[:, ct * T + ts * 512: ct * T + (ts + 1) * 512],
                              ct == 0, ct == NDT - 1,
                              inc=(sT if ct == NDT - 1 else None))
                if ts == 0:
                    wait(eng, qSp[dt % 2], pj_y[dt % 2])
                wait(eng, sT, last)
                if sigmoid:
                    pj_ev[r] = (sS, op("S", "activation", sb[:, ts * 512:(ts + 1) * 512],
                                       pp[:, :], AF.Sigmoid, inc=sS))
                else:
                    pj_ev[r] = (sV, op("V", "tensor_copy", sb[:, ts * 512:(ts + 1) * 512],
                                       pp[:, :], inc=sV))
            wait("Y", *pj_ev[pj_r[0] - 1])
            pj_y[dt % 2] = dma("Y", qSp[dt % 2], out_dram[:, dt * T:(dt + 1) * T],
                               sb[:, :])

    proj_dpart(Wsl1, (qW1, wk_ld), kT_d)
    dbg("wsl1", Wsl1[:, :], BF16)
    proj_dpart(Wsl0, (qW0, wx_ld), t1T_d)
    wr_ld = load_w(Wr_d, Wsl0, qW0, twait=sT.n)
    wv_ld = load_w(Wv_d, Wsl1, qW1, twait=sT.n)

    cwtr_ev = {}
    for tt in range(NT):
        half = (tt % 2) * 512
        wait("T", sV, aff_ev)
        if tt >= 2:
            wait("T", sV, cwtr_ev[tt - 2])
        atr = trans(pP0f[:, half:half + 64], affT_sb[:, tt * 128:(tt + 1) * 128],
                    identF[0:64, 0:64], inc=sT)
        wait("V", sT, atr)
        op("V", "tensor_copy", aff_sb[:, tt * NE:(tt + 1) * NE], pP0f[:, half:half + 64])
        a_t = aff_sb[:, tt * NE:(tt + 1) * NE]
        op("V", "tensor_tensor", bias_t[:, :], a_t, eb_bc[:, :], ALU.add)
        mx = op("V", "max", out=m8[:, 0:8], in_=bias_t[:, :], inc=sV)
        wait("V", sV, mx)
        ws_ = op("V", "tensor_scalar", wsel[:, :], bias_t[:, :], m8[:, 7:8], None,
                 ALU.is_ge, inc=sV)
        wait("S", sV, ws_)
        sg_ = op("S", "activation", sgm[:, :], a_t, AF.Sigmoid, inc=sS)
        wait("V", sS, sg_)
        op("V", "tensor_tensor", wsel[:, :], wsel[:, :], sgm[:, :], ALU.mult)
        op("V", "reduce_sum", cols[0][:, 0:1], wsel[:, :], AX.X)
        op("V", "tensor_scalar", cols[0][:, 1:2], cols[0][:, 0:1], 1e-8, None, ALU.add)
        op("V", "reciprocal", cols[0][:, 2:3], cols[0][:, 1:2])
        cwv = op("V", "tensor_scalar", cw_t[:, :], wsel[:, :], cols[0][:, 2:3], None,
                 ALU.mult, inc=sV)
        wait("T", sV, cwv)
        ctr = trans(pP0f[0:64, half + 64:half + 192], cw_t[:, :], identF[:, :], inc=sT)
        wait("V", sT, ctr)
        cwtr_ev[tt] = op("V", "tensor_copy", cwT[:, tt * 128:(tt + 1) * 128],
                         pP0f[0:64, half + 64:half + 192], inc=sV)

    wait("Y", sV, sV.n)
    cw_store = dma("Y", qRow, cwT_d, cwT[:, :])
    dbg("aff", aff_sb[:, :], F32)
    dbg("cwT", cwT[:, :], F32)


    proj_dpart(Wsl0, (qW0, wr_ld), rT_d, sigmoid=True)

    wait("T", qW1, wv_ld)
    wait("T", sS, sS.n)
    wait("T", sV, sV.n)
    wait("V", qSp[0], pj_y[0])
    wait("V", qSp[1], pj_y[1])
    for tt in range(NT):
        sb = spl[tt % 2]
        for ts in range(2):
            r = pj_r[0]
            pj_r[0] += 1
            pp = pWs[r % 2]
            wait("T", *pj_ev[r - 2])
            last = None
            for ct in range(NDT):
                lhs = xn2T[:, ct * T + tt * 128: ct * T + (tt + 1) * 128]
                last = mm(pp[:, :], lhs,
                          Wsl1[:, ct * 1024 + ts * 512: ct * 1024 + (ts + 1) * 512],
                          ct == 0, ct == NDT - 1,
                          inc=(sT if ct == NDT - 1 else None))
            if ts == 0:
                wait("V", qSv[tt % 2], vj_y[tt % 2])
            wait("V", sT, last)
            pj_ev[r] = (sV, op("V", "tensor_copy", sb[:, ts * 512:(ts + 1) * 512],
                               pp[:, :], inc=sV))
        wait("Y", *pj_ev[pj_r[0] - 1])
        vj_y[tt % 2] = dma("Y", qSv[tt % 2], vsp_d[:, tt * 1024:(tt + 1) * 1024],
                           sb[:, 0:1024])

    ww_ld = load_w(Ww_d, Wsl0, qW0, twait=sT.n)

    dbg("t1Td", t1T_d[:, :], BF16)
    dbg("kTd", kT_d[:, :], BF16)
    dbg("rTd", rT_d[:, :], BF16)
    dbg("vspd", vsp_d[:, :], BF16)
    wait("T", sV, sV.n)
    wait("T", sS, sS.n)
    a1p.close()
    p0.close()
    s01.close()
    sSPL.close()
    sW1.close()

    if "2" not in phases:
        return _finish(nc, all_stacks, L, sO, dbg_info)

    a2 = mkstack()
    E2 = a2.enter_context
    o_tl = [E2(nc.sbuf_tensor(f"o_tl{i}", [128, 1024], BF16, side="right")) for i in range(2)]
    t1g = [E2(nc.sbuf_tensor(f"t1g{i}", [128, NDT * 256], BF16, side="right")) for i in range(2)]
    v_tl = [E2(nc.sbuf_tensor(f"v_tl{i}", [128, 1024], BF16, side="right")) for i in range(2)]
    kt_sl = [E2(nc.sbuf_tensor(f"kt_sl{i}", [128, 1024], BF16, side="right")) for i in range(2)]
    rt_sl = [E2(nc.sbuf_tensor(f"rt_sl{i}", [128, 1024], BF16, side="right")) for i in range(2)]
    STg = [E2(nc.sbuf_tensor(f"STg{i}", [128, NDT * 256], F32, side="right")) for i in range(2)]
    sp_u = [E2(nc.sbuf_tensor(f"sp_u{i}", [128, 512], F32, side="right")) for i in range(2)]
    sp_e = [E2(nc.sbuf_tensor(f"sp_e{i}", [128, 512], F32, side="right")) for i in range(2)]
    sp_l = [E2(nc.sbuf_tensor(f"sp_l{i}", [128, 512], F32, side="right")) for i in range(2)]
    sp_r = [E2(nc.sbuf_tensor(f"sp_r{i}", [128, 512], F32, side="right")) for i in range(2)]
    sftp = [E2(nc.sbuf_tensor(f"sftp{i}", [128, 512], F32, side="right")) for i in range(2)]
    bcols = [E2(nc.sbuf_tensor(f"bcols{i}", [128, 64], F32, side="right")) for i in range(2)]
    E_eb = [E2(nc.sbuf_tensor(f"E_eb{i}", [128, 1024], BF16, side="right")) for i in range(2)]
    E_kh = [E2(nc.sbuf_tensor(f"E_kh{i}", [128, 1024], BF16, side="right")) for i in range(2)]
    E_kt = [E2(nc.sbuf_tensor(f"E_kt{i}", [128, 1024], BF16, side="right")) for i in range(2)]
    E_r0 = [E2(nc.sbuf_tensor(f"E_r0{i}", [128, 1024], BF16, side="right")) for i in range(2)]
    E_r1 = [E2(nc.sbuf_tensor(f"E_r1{i}", [128, 512], BF16, side="right")) for i in range(2)]
    clampb = E2(nc.sbuf_tensor("clampb", [128, 2048], BF16, side="right"))
    khT = [[E2(nc.sbuf_tensor(f"khT{i}{j}", [128, 1024], BF16, side="right")) for j in range(2)]
           for i in range(2)]
    ktlT = [[E2(nc.sbuf_tensor(f"ktlT{i}{j}", [128, 1024], BF16, side="right")) for j in range(2)]
            for i in range(2)]
    rt0T = [[E2(nc.sbuf_tensor(f"rt0T{i}{j}", [128, 1024], BF16, side="right")) for j in range(2)]
            for i in range(2)]
    rt1T = [[E2(nc.sbuf_tensor(f"rt1T{i}{j}", [128, 512], BF16, side="right")) for j in range(2)]
            for i in range(2)]
    rhT = [[E2(nc.sbuf_tensor(f"rhT{i}{j}", [128, 1024], BF16, side="right")) for j in range(2)]
           for i in range(2)]
    khtp = [E2(nc.sbuf_tensor(f"khtp{j}", [128, 1024], BF16, side="right")) for j in range(2)]
    Am = E2(nc.sbuf_tensor("Am", [128, 2048], BF16, side="right"))
    state = E2(nc.sbuf_tensor("state", [128, 512], F32, side="right"))
    state_bf = E2(nc.sbuf_tensor("state_bf", [128, 512], BF16, side="right"))
    edec = [E2(nc.sbuf_tensor(f"edec{i}", [128, 8], F32, side="right")) for i in range(2)]

    pz2 = [E2(nc.psum_tensor(f"pz{i}", [128, 512], F32)) for i in range(2)]
    pAtr = E2(nc.psum_tensor("pAtr", [128, 1024], F32))
    pOut = E2(nc.psum_tensor("pOut", [128, 1024], F32))
    pSup = E2(nc.psum_tensor("pSup", [128, 512], F32))
    pKtp = pSup[:, :].bitcast(BF16)

    barrier("Y")
    wait("Y", qSp[0], qSp[0].n)
    wait("Y", qSp[1], qSp[1].n)
    wait("Y", qSv[0], qSv[0].n)
    wait("Y", qSv[1], qSv[1].n)
    wait("V", qSp[0], qSp[0].n)
    wait("V", qSp[1], qSp[1].n)
    wait("V", qSv[0], qSv[0].n)
    wait("V", qSv[1], qSv[1].n)
    op("V", "memset", state[:, :], 0.0)
    op("V", "memset", state_bf[:, :], 0.0)
    for i in range(2):
        for j in range(2):
            op("V", "memset", ktlT[i][j][:, :], 0.0)
    a2_init = op("V", "memset", pAtr[:, :], 0.0, inc=sV)
    wait("T", sV, a2_init)

    ST3g = [STg[i][:, :].rearrange("p (c u) -> p c u", u=256) for i in range(2)]
    kT_r = kT_d.rearrange("p (c u) -> p c u", u=T)
    rT_r = rT_d.rearrange("p (c u) -> p c u", u=T)

    z_T = {}
    z_S = {-1: 0, -2: 0}
    scan_V = {-1: 0, -2: 0}
    last_oev = 0
    last_stV = [a2_init]
    last_cbv = [a2_init]
    last_khtpS = 0
    av_T = {-1: 0, -2: 0}
    ktv_tile = {}
    amm0_tile = {}
    amm1_tile = {}
    cb0_tile = {}
    m0_tile = {}
    m1_tile = {}
    sup_tile = {}
    tile_T = {-1: 0, -2: 0}
    scS_tile = {-1: 0, -2: 0}
    v1_tile = {}
    stV_tile = {-1: 0, -2: 0}
    o_y = {}
    t1_y = {}
    vt_y = {}
    kt_y = {}
    rt_y = {}
    slice_free_V = {-2: 0, -1: 0}
    t1_free_T = {}
    v_free_T = {}

    def load_slices(tt):
        if tt >= NT:
            return
        wait("Y", sV, slice_free_V[tt - 2])
        kt_y[tt] = dma("Y", qKt[tt % 2], kt_sl[tt % 2][:, :].rearrange("p (c u) -> p c u", u=128),
                       kT_r[:, :, tt * 128:(tt + 1) * 128])
        rt_y[tt] = dma("Y", qRt[tt % 2], rt_sl[tt % 2][:, :].rearrange("p (c u) -> p c u", u=128),
                       rT_r[:, :, tt * 128:(tt + 1) * 128])

    def emit_z_step(g_, dtp):
        gp = g_ % 2
        stepp = g_ * (NDT // 2) + dtp
        par = stepp % 2
        ppz = pz2[par]
        if dtp == 0:
            if g_ >= 2:
                wait("Y", sT, t1_free_T[g_ - 2])
            t1_y[g_] = dma("Y", qT1g[gp], t1g[gp][:, :].rearrange("p (c u) -> p c u", u=256),
                           t1T_d.rearrange("p (c u) -> p c u", u=T)[:, :, g_ * 256:(g_ + 1) * 256])
        wait("T", sS, z_S[stepp - 2])
        wait("T", qT1g[gp], t1_y[g_])
        if stepp == 0:
            wait("T", qW0, ww_ld)
        last = None
        for half in range(2):
            dt = 2 * dtp + half
            for ct in range(NDT):
                last = mm(ppz[:, half * 256: half * 256 + 256],
                          Wsl0[:, ct * 1024 + dt * 128: ct * 1024 + dt * 128 + 128],
                          t1g[gp][:, ct * 256:(ct + 1) * 256], ct == 0, ct == NDT - 1,
                          inc=(sT if (half == 1 and ct == NDT - 1) else None))
        z_T[stepp] = last
        wait("S", sT, last)
        wait("S", sV, scan_V[stepp - 2])
        for half in range(2):
            dt = 2 * dtp + half
            bwc = bw_sb[:, dt:dt + 1]
            sl = slice(half * 256, half * 256 + 256)
            op("S", "activation", sp_u[par][:, sl], ppz[:, sl], AF.Abs, bias=bwc)
            op("S", "activation", sp_e[par][:, sl], sp_u[par][:, sl], AF.Exp, scale=-1.0)
            op("S", "activation", sp_l[par][:, sl], sp_e[par][:, sl], AF.Ln, bias=1.0)
            z_S[stepp] = op("S", "activation", sp_r[par][:, sl], ppz[:, sl], AF.Relu,
                            bias=bwc, inc=(sS if half == 1 else None))
        if dtp == NDT // 2 - 1:
            t1_free_T[g_] = z_T[stepp]

    def emit_scans(g_, dtp):
        gp = g_ % 2
        stepp = g_ * (NDT // 2) + dtp
        par = stepp % 2
        wait("V", sS, z_S[stepp])
        if g_ >= 2 and dtp == 0:
            wait("V", sS, scS_tile[2 * (g_ - 2) + 1])
        first = True
        for half in range(2):
            dt = 2 * dtp + half
            for tl in range(2):
                sl = slice(half * 256 + tl * 128, half * 256 + (tl + 1) * 128)
                scan_V[stepp] = op("V", "tensor_tensor_scan",
                                   STg[gp][:, dt * 256 + tl * 128: dt * 256 + (tl + 1) * 128],
                                   sp_r[par][:, sl], sp_l[par][:, sl], 0.0,
                                   ALU.add, ALU.add, chain=first,
                                   inc=(sV if (half == 1 and tl == 1) else None))
                first = False

    def argsV(tt):
        g_ = tt // 2
        tl = tt % 2
        tp_ = tt % 2
        gp = g_ % 2
        ST3 = ST3g[gp]
        wait("V", sV, scan_V[g_ * (NDT // 2) + NDT // 2 - 1])
        wait("V", sS, scS_tile[tt - 2])
        op("V", "tensor_copy", bcols[tp_][:, 0:8], ST3[:, :, tl * 128 + 31: tl * 128 + 32])
        op("V", "tensor_copy", bcols[tp_][:, 8:16], ST3[:, :, tl * 128 + 95: tl * 128 + 96],
           chain=False)
        op("V", "tensor_copy", bcols[tp_][:, 16:24],
           ST3[:, :, tl * 128 + 127: tl * 128 + 128], chain=False)
        v1 = op("V", "tensor_scalar", bcols[tp_][:, 32:56], bcols[tp_][:, 0:24],
                -1.0, None, ALU.mult, inc=sV)
        v1_tile[tt] = v1

    def expsS(tt):
        g_ = tt // 2
        tl = tt % 2
        tp_ = tt % 2
        gp = g_ % 2
        ST3 = ST3g[gp]
        v1 = v1_tile[tt]
        wait("S", sV, v1)
        wait("S", sV, stV_tile[tt - 2])
        op("S", "activation", edec[tp_][:, :], bcols[tp_][:, 16:24], AF.Exp, scale=-1.0)
        op("S", "activation", E_eb[tp_][:, :].rearrange("p (c u) -> p c u", u=128),
           ST3[:, :, tl * 128:(tl + 1) * 128], AF.Exp, scale=-1.0)
        for dt in range(NDT):
            op("S", "activation", E_kh[tp_][:, dt * 128:(dt + 1) * 128],
               ST3[:, dt, tl * 128:(tl + 1) * 128], AF.Exp,
               bias=bcols[tp_][:, 48 + dt:49 + dt])
        for half in range(2):
            for dt in range(NDT):
                op("S", "activation",
                   E_kt[tp_][:, dt * 128 + half * 64: dt * 128 + half * 64 + 64],
                   ST3[:, dt, tl * 128 + half * 64: tl * 128 + half * 64 + 64],
                   AF.Exp, bias=bcols[tp_][:, 32 + half * 8 + dt: 33 + half * 8 + dt])
        scS = None
        for dt in range(NDT):
            op("S", "activation", E_r0[tp_][:, dt * 128:(dt + 1) * 128],
               ST3[:, dt, tl * 128:(tl + 1) * 128], AF.Exp, scale=-1.0,
               bias=bcols[tp_][:, 0 + dt:1 + dt])
            scS = op("S", "activation", E_r1[tp_][:, dt * 64:(dt + 1) * 64],
                     ST3[:, dt, tl * 128 + 64:tl * 128 + 128], AF.Exp, scale=-1.0,
                     bias=bcols[tp_][:, 8 + dt:9 + dt],
                     inc=(sS if dt == NDT - 1 else None))
        scS_tile[tt] = scS

    def stageV2(tt):
        tp_ = tt % 2
        wait("V", sS, scS_tile[tt])
        wait("V", qKt[tp_], kt_y[tt])
        wait("V", qRt[tp_], rt_y[tt])
        wait("V", sT, av_T[tt - 2])
        for par in range(2):
            op("V", "scalar_tensor_tensor", rhT[tp_][par][:, :],
               rt_sl[tp_][:, :], mask2[:, par:par + 1], E_eb[tp_][:, :],
               ALU.mult, ALU.mult, chain=(par == 0))
            op("V", "scalar_tensor_tensor", khT[tp_][par][:, :],
               kt_sl[tp_][:, :], mask2[:, par:par + 1], E_kh[tp_][:, :],
               ALU.mult, ALU.mult, chain=False)
            op("V", "scalar_tensor_tensor", rt0T[tp_][par][:, :],
               rt_sl[tp_][:, :], mask2[:, par:par + 1], E_r0[tp_][:, :],
               ALU.mult, ALU.mult, chain=False)
            op("V", "scalar_tensor_tensor", rt1T[tp_][par][:, :],
               rt_sl[tp_][:, :].rearrange("p (c u) -> p c u", u=128)[:, :, 64:128],
               mask2[:, par:par + 1],
               E_r1[tp_][:, :].rearrange("p (c u) -> p c u", u=64),
               ALU.mult, ALU.mult, chain=False)
        ktv = None
        for half in range(2):
            ktv = op("V", "tensor_tensor",
                     ktlT[tp_][half][:, :].rearrange("p (c u) -> p c u", u=128)[:, :, half * 64:(half + 1) * 64],
                     kt_sl[tp_][:, :].rearrange("p (c u) -> p c u", u=128)[:, :, half * 64:(half + 1) * 64],
                     E_kt[tp_][:, :].rearrange("p (c u) -> p c u", u=128)[:, :, half * 64:(half + 1) * 64],
                     ALU.mult, chain=False, inc=(sV if half == 1 else None))
        slice_free_V[tt] = ktv
        ktv_tile[tt] = ktv
        load_slices(tt + 2)

    def stageT_a(tt):
        nonlocal last_khtpS
        tp_ = tt % 2
        wait("T", sV, ktv_tile[tt])
        wait("T", sV, stV_tile[tt - 1])
        for par in range(2):
            wait("T", sS, last_khtpS)
            last = None
            for dt in range(NDT):
                last = trans(pKtp[:, dt * 128:(dt + 1) * 128],
                             khT[tp_][par][:, dt * 128:(dt + 1) * 128],
                             identB[:, :], inc=(sT if dt == NDT - 1 else None))
            wait("S", sT, last)
            last_khtpS = op("S", "activation", khtp[par][:, :], pKtp[:, :],
                            AF.Copy, inc=sS)
        wait("T", sS, last_khtpS)
        wait("T", sV, last_cbv[0])
        last = None
        for h in range(8):
            dt = h // 2
            par = h % 2
            mm(pAtr[:, h * 128:(h + 1) * 128],
               ktlT[tp_][0][:, dt * 128:(dt + 1) * 128],
               rt0T[tp_][par][:, dt * 128:(dt + 1) * 128],
               True, False)
            last = mm(pAtr[:, h * 128 + 64:(h + 1) * 128],
                      ktlT[tp_][1][:, dt * 128:(dt + 1) * 128],
                      rt1T[tp_][par][:, dt * 64:(dt + 1) * 64],
                      False, True, inc=(sT if h == 7 else None))
        amm0_tile[tt] = last
        wait("T", sV, last_stV[0])
        last = None
        for dt in range(NDT):
            mm(pSup[:, dt * 64:(dt + 1) * 64],
               khtp[0][:, dt * 128:(dt + 1) * 128],
               v_tl[tp_][:, (2 * dt) * 64:(2 * dt + 1) * 64],
               True, False)
            last = mm(pSup[:, dt * 64:(dt + 1) * 64],
                      khtp[1][:, dt * 128:(dt + 1) * 128],
                      v_tl[tp_][:, (2 * dt + 1) * 64:(2 * dt + 2) * 64],
                      False, True, inc=(sT if dt == NDT - 1 else None))
        sup_tile[tt] = last
        tile_T[tt] = last

    def clampmask0(tt):
        wait("V", sT, amm0_tile[tt])
        cb0_tile[tt] = op("V", "tensor_scalar", clampb[:, 0:1024], pAtr[:, :],
                          1e30, -1e30, ALU.min, ALU.max, inc=sV)
        m0_tile[tt] = op("V", "tensor_tensor", Am[:, 0:1024], clampb[:, 0:1024],
                         maskA[:, 0:1024], ALU.mult, inc=sV)

    def stageT_b(tt):
        tp_ = tt % 2
        wait("T", sV, cb0_tile[tt])
        last = None
        for h in range(8, NH):
            dt = h // 2
            par = h % 2
            mm(pAtr[:, (h - 8) * 128:(h - 7) * 128],
               ktlT[tp_][0][:, dt * 128:(dt + 1) * 128],
               rt0T[tp_][par][:, dt * 128:(dt + 1) * 128],
               True, False)
            last = mm(pAtr[:, (h - 8) * 128 + 64:(h - 7) * 128],
                      ktlT[tp_][1][:, dt * 128:(dt + 1) * 128],
                      rt1T[tp_][par][:, dt * 64:(dt + 1) * 64],
                      False, True, inc=(sT if h == NH - 1 else None))
        amm1_tile[tt] = last

    def clampmask1(tt):
        wait("V", sT, amm1_tile[tt])
        last_cbv[0] = op("V", "tensor_scalar", clampb[:, 1024:2048], pAtr[:, :],
                         1e30, -1e30, ALU.min, ALU.max, inc=sV)
        m1_tile[tt] = op("V", "tensor_tensor", Am[:, 1024:2048],
                         clampb[:, 1024:2048], maskA[:, 1024:2048],
                         ALU.mult, inc=sV)

    def stageT_c(tt):
        nonlocal last_oev
        tp_ = tt % 2
        wait("T", sV, m0_tile[tt])
        wait("T", qVt[tp_], vt_y[tt])
        wait("T", sS, last_oev)
        for h in range(8):
            dt = h // 2
            par = h % 2
            mm(pOut[:, h * 64:(h + 1) * 64],
               Am[:, h * 128:(h + 1) * 128],
               v_tl[tp_][:, h * 64:(h + 1) * 64],
               True, tt == 0)
            if tt > 0:
                mm(pOut[:, h * 64:(h + 1) * 64],
                   rhT[tp_][par][:, dt * 128:(dt + 1) * 128],
                   state_bf[:, dt * 64:(dt + 1) * 64],
                   False, True)
        wait("T", sV, m1_tile[tt])
        av_last = None
        for h in range(8, NH):
            dt = h // 2
            par = h % 2
            is_last = (h == NH - 1)
            av_last = mm(pOut[:, h * 64:(h + 1) * 64],
                         Am[:, 1024 + (h - 8) * 128:1024 + (h - 7) * 128],
                         v_tl[tp_][:, h * 64:(h + 1) * 64],
                         True, tt == 0,
                         inc=(sT if (is_last and tt == 0) else None))
            if tt > 0:
                av_last = mm(pOut[:, h * 64:(h + 1) * 64],
                             rhT[tp_][par][:, dt * 128:(dt + 1) * 128],
                             state_bf[:, dt * 64:(dt + 1) * 64],
                             False, True, inc=(sT if is_last else None))
        av_T[tt] = av_last
        v_free_T[tt] = av_last
        wait("S", sT, av_last)
        if tt >= 2:
            wait("S", qOst[tp_], o_y[tt - 2])
        last_oev = op("S", "activation", o_tl[tp_][:, :], pOut[:, :], AF.Copy, inc=sS)
        wait("Y", sS, last_oev)
        o_y[tt] = dma("Y", qOst[tp_], o_d[:, tt * 1024:(tt + 1) * 1024], o_tl[tp_][:, :])

    def stateupd(tt):
        tp_ = tt % 2
        wait("V", sT, sup_tile[tt])
        wait("V", sS, scS_tile[tt])
        for dt in range(NDT):
            op("V", "scalar_tensor_tensor", state[:, dt * 64:(dt + 1) * 64],
               state[:, dt * 64:(dt + 1) * 64], edec[tp_][:, dt:dt + 1],
               pSup[:, dt * 64:(dt + 1) * 64], ALU.mult, ALU.add, chain=(dt == 0))
        stv = op("V", "tensor_copy", state_bf[:, :], state[:, :], inc=sV)
        last_stV[0] = stv
        stV_tile[tt] = stv

    def v_load(tt):
        tp_ = tt % 2
        if tt >= 2:
            wait("Y", sT, v_free_T[tt - 2])
        vt_y[tt] = dma("Y", qVt[tp_], v_tl[tp_][:, :], vsp_d[:, tt * 1024:(tt + 1) * 1024])

    NG = NT // 2
    load_slices(0)
    load_slices(1)
    for g in range(2):
        for dq in range(NDT // 2):
            emit_z_step(g, dq)
            emit_scans(g, dq)
    v_load(0)
    argsV(0)
    expsS(0)
    stageV2(0)
    pend = []
    for tt in range(NT):
        if tt + 1 < NT:
            v_load(tt + 1)
            argsV(tt + 1)
        for (gq, dq) in pend:
            emit_scans(gq, dq)
        pend = []
        stageT_a(tt)
        clampmask0(tt)
        stageT_b(tt)
        clampmask1(tt)
        if tt + 1 < NT:
            expsS(tt + 1)
        stageT_c(tt)
        stateupd(tt)
        if tt + 1 < NT:
            stageV2(tt + 1)
        g_n = tt // 2 + 2
        if g_n < NG:
            for dq in ((0, 1) if tt % 2 == 0 else (2, 3)):
                emit_z_step(g_n, dq)
                pend.append((g_n, dq))

    wait("T", sV, sV.n)
    wait("Y", qOst[0], qOst[0].n)
    wait("Y", qOst[1], qOst[1].n)
    dbg("o_d", o_d[:, :], BF16)
    dbg("Am15", Am[:, :], BF16)
    dbg("v15", v_tl[1][:, :], BF16)
    dbg("st15", state_bf[:, :], BF16)
    dbg("kh15", E_kh[1][:, :], BF16)
    dbg("kt15", kt_sl[1][:, :], BF16)
    dbg("vsp", vsp_d[:, :], BF16)
    a2.close()
    sW0.close()

    accT = E(nc.sbuf_tensor("accT", [128, NDT * T], F32))
    a3 = mkstack()
    oT = a3.enter_context(nc.sbuf_tensor("oT", [128, NDT * T], BF16, side="right"))
    Wo_sl = a3.enter_context(nc.sbuf_tensor("Wo_sl", [128, NDT * 1024], BF16, side="right"))
    o_tl3 = [a3.enter_context(nc.sbuf_tensor(f"o_tl3{i}", [128, 1024], BF16, side="right"))
             for i in range(2)]
    a3t = mkstack()
    pT3 = [a3t.enter_context(nc.psum_tensor(f"pT3{i}", [128, 1024], BF16)) for i in range(2)]

    barrier("G")
    barrier("Y")
    wait("Y", qOst[0], qOst[0].n)
    wait("Y", qOst[1], qOst[1].n)
    wo_ld = load_w(Wo_d, Wo_sl, qWoS, twait=0)
    otr = {}
    oy3 = {}
    otr_T = {}
    for tt in range(NT):
        pp = pT3[tt % 2]
        if tt >= 2:
            wait("Y", sT, otr_T[tt - 2])
        oy3[tt] = dma("Y", qOl3[tt % 2], o_tl3[tt % 2][:, :], o_d[:, tt * 1024:(tt + 1) * 1024])
        if tt >= 2:
            wait("T", sV, otr[tt - 2])
        wait("T", qOl3[tt % 2], oy3[tt])
        last = None
        for c in range(NDT):
            last = trans(pp[:, c * 128:(c + 1) * 128],
                         o_tl3[tt % 2][:, c * 128:(c + 1) * 128],
                         identB[:, :], inc=(sT if c == NDT - 1 else None))
        otr_T[tt] = last
        wait("V", sT, last)
        otr[tt] = op("V", "tensor_copy",
                     oT[:, :].rearrange("p (c u) -> p c u", u=T)[:, :, tt * 128:(tt + 1) * 128],
                     pp[:, :].rearrange("p (c u) -> p c u", u=128), inc=sV)

    a3t.close()
    a3p = mkstack()
    pW3 = [a3p.enter_context(nc.psum_tensor(f"pW3{i}", [128, 2048], F32)) for i in range(2)]
    wait("T", qWoS, wo_ld)
    wait("T", sV, sV.n)
    wo_ev = {}
    for dt in range(NDT):
        pp = pW3[dt % 2]
        if dt >= 2:
            wait("T", sV, wo_ev[dt - 2])
        last = None
        for ct in range(NDT):
            lhs = Wo_sl[:, ct * 1024 + dt * 128: ct * 1024 + dt * 128 + 128]
            for ts in range(4):
                last = mm(pp[:, ts * 512:(ts + 1) * 512], lhs,
                          oT[:, ct * T + ts * 512: ct * T + (ts + 1) * 512],
                          ct == 0, ct == NDT - 1,
                          inc=(sT if (ct == NDT - 1 and ts == 3) else None))
        wait("V", sT, last)
        wo_ev[dt] = op("V", "tensor_copy", accT[:, dt * T:(dt + 1) * T], pp[:, :], inc=sV)

    dbg("accT_attn", accT[:, :], F32)
    wait("T", sV, sV.n)
    a3.close()
    a3p.close()

    if "b" not in phases:
        return _finish(nc, all_stacks, L, sO, dbg_info)

    b_ = mkstack()
    EB = b_.enter_context
    hTs = EB(nc.sbuf_tensor("hTs", [128, EBLK * NFB * 1024], BF16, side="right"))
    Wup_sl = [EB(nc.sbuf_tensor(f"Wup{i}", [128, NDT * 512], BF16, side="right")) for i in range(EBLK)]
    Wdn_sl = [EB(nc.sbuf_tensor(f"Wdn{i}", [128, NFB * 1024], BF16, side="right")) for i in range(EBLK)]
    cwbc = EB(nc.sbuf_tensor("cwbc", [128, EBLK * T], BF16, side="right"))
    pUp = [EB(nc.psum_tensor(f"pUp{i}", [128, 1024], F32)) for i in range(2)]
    pDn = [EB(nc.psum_tensor(f"pDn{i}", [128, 1024], F32)) for i in range(2)]

    experts = [-1] + list(range(NE))
    blocks = [experts[i:i + EBLK] for i in range(0, len(experts), EBLK)]

    def wup_load(e, slot, twait=0):
        wait("G", sT, twait)
        src = Wsu_d if e < 0 else Wup_d[e]
        return dma("G", qUp[slot], Wup_sl[slot][:, :].rearrange("p (c u) -> p c u", u=512),
                   src.rearrange("(c p) u -> p c u", p=128))

    def wdn_load(e, slot, twait=0):
        wait("G", sT, twait)
        src = Wsd_d if e < 0 else Wdn_d[e]
        return dma("G", qDn[slot], Wdn_sl[slot][:, :].rearrange("p (c u) -> p c u", u=1024),
                   src.rearrange("(c p) u -> p c u", p=128))

    up_ld = {}
    dn_ld = {}
    bc_ld = {}
    up_slot_T = [0] * EBLK
    bc_war = [0] * EBLK
    last_pUpS = [0, 0]
    last_pUpV = [0, 0]
    last_pDnV = [0, 0]
    last_dn_T = 0
    upr = 0
    h_done = {}

    barrier("G")
    barrier("Y")
    for ei, e in enumerate(blocks[0]):
        up_ld[(0, ei)] = wup_load(e, ei)
        dn_ld[(0, ei)] = wdn_load(e, ei)

    for bi, blk_ in enumerate(blocks):
        nb = len(blk_)
        for ei, e in enumerate(blk_):
            if e < 0:
                op("V", "memset", cwbc[:, ei * T:(ei + 1) * T], 1.0, inc=sV)
                bc_ld[(bi, ei)] = 0
                continue
            wait("G", sV, bc_war[ei])
            if bi == 0:
                wait("G", qRow, cw_store)
            bc_ld[(bi, ei)] = dma("G", qBc[ei], cwbc[:, ei * T:(ei + 1) * T],
                                  cwT_d[e:e + 1, :].partition_broadcast(128))
        for th in range(2):
            for ei, e in enumerate(blk_):
                wait("T", qUp[ei], up_ld[(bi, ei)])
                for fb in range(NFB):
                    reg = upr % 2
                    upr += 1
                    pp = pUp[reg]
                    wait("T", sS, last_pUpS[reg])
                    wait("T", sV, last_pUpV[reg])
                    last = None
                    for ct in range(NDT):
                        lhs = Wup_sl[ei][:, ct * 512 + fb * 128: ct * 512 + fb * 128 + 128]
                        for ts in range(2):
                            last = mm(pp[:, ts * 512:(ts + 1) * 512], lhs,
                                      xnT[:, ct * T + th * 1024 + ts * 512:
                                          ct * T + th * 1024 + (ts + 1) * 512],
                                      ct == 0, ct == NDT - 1,
                                      inc=(sT if (ct == NDT - 1 and ts == 1) else None))
                    up_slot_T[ei] = last
                    h_sl = hTs[:, ei * 4096 + fb * 1024: ei * 4096 + (fb + 1) * 1024]
                    wait("S", sT, last)
                    wait("S", sT, last_dn_T)
                    sS_ = op("S", "activation", h_sl, pp[:, :], AF.Silu, inc=sS)
                    last_pUpS[reg] = sS_
                    wait("V", sS, sS_)
                    if bc_ld[(bi, ei)]:
                        wait("V", qBc[ei], bc_ld[(bi, ei)])
                    hv = op("V", "tensor_tensor", h_sl, h_sl,
                            cwbc[:, ei * T + th * 1024: ei * T + (th + 1) * 1024],
                            ALU.mult, inc=sV)
                    bc_war[ei] = hv
                    h_done[(ei, fb)] = hv
            for ei in range(nb):
                wait("T", qDn[ei], dn_ld[(bi, ei)])
            for dt in range(NDT):
                reg = dt % 2
                pp = pDn[reg]
                wait("T", sV, last_pDnV[reg])
                last = None
                for ei in range(nb):
                    for fb in range(NFB):
                        if dt == 0:
                            wait("T", sV, h_done[(ei, fb)])
                        lhs = Wdn_sl[ei][:, fb * 1024 + dt * 128: fb * 1024 + dt * 128 + 128]
                        for ts in range(2):
                            stop = (ei == nb - 1 and fb == NFB - 1)
                            last = mm(pp[:, ts * 512:(ts + 1) * 512], lhs,
                                      hTs[:, ei * 4096 + fb * 1024 + ts * 512:
                                          ei * 4096 + fb * 1024 + (ts + 1) * 512],
                                      (ei == 0 and fb == 0), stop,
                                      inc=(sT if (stop and ts == 1) else None))
                wait("V", sT, last)
                acc_sl = accT[:, dt * T + th * 1024: dt * T + (th + 1) * 1024]
                last_pDnV[reg] = op("V", "tensor_tensor", acc_sl, acc_sl, pp[:, :],
                                    ALU.add, inc=sV)
            last_dn_T = sT.n
        if bi + 1 < len(blocks):
            for ei, e in enumerate(blocks[bi + 1]):
                up_ld[(bi + 1, ei)] = wup_load(e, ei, twait=up_slot_T[ei])
                dn_ld[(bi + 1, ei)] = wdn_load(e, ei, twait=last_dn_T)

    dbg("accT", accT[:, :], F32)
    wait("T", sV, sV.n)
    b_.close()

    if "c" not in phases:
        return _finish(nc, all_stacks, L, sO, dbg_info)

    c_ = mkstack()
    EC = c_.enter_context
    xcb = [EC(nc.sbuf_tensor(f"xcb{i}", [128, D], F32, side="right")) for i in range(2)]
    outb = [EC(nc.sbuf_tensor(f"outb{i}", [128, D], F32, side="right")) for i in range(2)]
    pC = [EC(nc.psum_tensor(f"pC{i}", [128, 1024], F32)) for i in range(2)]

    barrier("Y")
    ctr_ev = {}
    x_y = {}
    out_y = {}

    def c_load(tt):
        if tt >= NT:
            return
        if tt >= 2:
            wait("Y", sV, ctr_ev[tt - 2])
        x_y[tt] = dma("Y", qX[tt % 2], xcb[tt % 2][:, :], xin[tt * 128:(tt + 1) * 128, :])

    c_load(0)
    for tt in range(NT):
        pp = pC[tt % 2]
        c_load(tt + 1)
        if tt >= 2:
            wait("T", sV, ctr_ev[tt - 2])
        last = None
        for dt in range(NDT):
            last = trans(pp[:, dt * 128:(dt + 1) * 128],
                         accT[:, dt * T + tt * 128: dt * T + (tt + 1) * 128],
                         identF[:, :], inc=(sT if dt == NDT - 1 else None))
        wait("V", sT, last)
        wait("V", qX[tt % 2], x_y[tt])
        if tt >= 2:
            wait("V", qOut[tt % 2], out_y[tt - 2])
        ctr_ev[tt] = op("V", "tensor_tensor", outb[tt % 2][:, :], pp[:, 0:D],
                        xcb[tt % 2][:, :], ALU.add, inc=sV)
        wait("Y", sV, ctr_ev[tt])
        out_y[tt] = dma("Y", qOut[tt % 2], out_d[tt * 128:(tt + 1) * 128, :], outb[tt % 2][:, :])
    wait("Y", qOut[0], qOut[0].n)
    wait("Y", qOut[1], qOut[1].n)
    c_.close()

    return _finish(nc, all_stacks, L, sO, dbg_info)


def _finish(nc, all_stacks, L, sO, dbg_info):
    L["Y"].append(lambda g, h=sO.h, n=sO.n: g.wait_ge(h, n))
    with nc.Block() as blk:
        @blk.tensor
        def _(t):
            for f in L["T"]:
                f(t)

        @blk.vector
        def _(v):
            for f in L["V"]:
                f(v)

        @blk.scalar
        def _(s):
            for f in L["S"]:
                f(s)

        @blk.gpsimd
        def _(g):
            for f in L["G"]:
                f(g)

        @blk.sync
        def _(y):
            for f in L["Y"]:
                f(y)
    for s in reversed(all_stacks):
        s.close()
    return nc, dbg_info


def host_consts():
    identF = np.eye(128, dtype=np.float32)
    identB = identF.astype(ml_dtypes.bfloat16)
    tri = (np.arange(128)[None, :] >= np.arange(128)[:, None]).astype(np.float32)
    maskA = np.tile(tri, (1, 16)).astype(ml_dtypes.bfloat16)
    return identB, identF, maskA


def make_inmaps(L0_attn_x, L0_attn_ln_g, L0_attn_ln_b, L0_attn_Wx, L0_attn_Ww,
                L0_attn_bw, L0_attn_Wk, L0_attn_Wv, L0_attn_Wr, L0_attn_Wo,
                L0_ffn_centroids, L0_ffn_expert_biases, L0_ffn_Ws_up,
                L0_ffn_Ws_down, L0_ffn_W_up, L0_ffn_W_down):
    identB, identF, maskA = host_consts()
    f32 = lambda a: np.ascontiguousarray(np.asarray(a, dtype=np.float32))
    shared = dict(
        ln_g=f32(L0_attn_ln_g).reshape(1, D),
        ln_b=f32(L0_attn_ln_b).reshape(1, D),
        Wx=f32(L0_attn_Wx), Ww=f32(L0_attn_Ww),
        bw=np.ascontiguousarray(f32(L0_attn_bw).reshape(NDT, 128).T),
        Wk=f32(L0_attn_Wk), Wv=f32(L0_attn_Wv), Wr=f32(L0_attn_Wr), Wo=f32(L0_attn_Wo),
        cent=f32(L0_ffn_centroids),
        ebias=f32(L0_ffn_expert_biases).reshape(1, NE),
        Wsu=f32(L0_ffn_Ws_up), Wsd=f32(L0_ffn_Ws_down),
        Wup=f32(L0_ffn_W_up), Wdn=f32(L0_ffn_W_down),
        identB=identB, identF=identF, maskA=maskA,
    )
    xs = np.asarray(L0_attn_x, dtype=np.float32)
    return [dict(shared, x=np.ascontiguousarray(xs[i])) for i in range(xs.shape[0])]


def kernel(**inputs):
    nc, _ = build()
    in_maps = make_inmaps(**inputs)
    res = run_bass_kernel_spmd(nc, in_maps, core_ids=list(range(8)))
    return np.stack([np.asarray(res.results[i]["out"]) for i in range(8)], axis=0)
